# version 22
# baseline (speedup 1.0000x reference)
from contextlib import ExitStack

import numpy as np
import ml_dtypes
import concourse.bass as bass
import concourse.mybir as mybir
from concourse.bass_utils import run_bass_kernel_spmd

F32 = mybir.dt.float32
BF16 = mybir.dt.bfloat16
U8 = mybir.dt.uint8
ALU = mybir.AluOpType
AF = mybir.ActivationFunctionType
AX = mybir.AxisListType

T, C, H, D, I = 2048, 1024, 16, 64, 2816
NT = T // 128
CCH = C // 128
CV = float(2 ** 22 + 1)
CR = 12582912.0
IBLKS = [(0, 512), (512, 512), (1024, 512), (1536, 512), (2048, 512), (2560, 256)]


class PR:

    ENG = ("t", "v", "s", "g", "y")

    def __init__(self, nc):
        self.nc = nc
        self.eng = {"t": nc.tensor, "v": nc.vector, "s": nc.scalar,
                    "g": nc.gpsimd, "y": nc.sync}
        self.sem = {k: nc.alloc_semaphore(name=f"es_{k}") for k in ("t", "v", "s", "g")}
        self.cnt = {k: 0 for k in ("t", "v", "s", "g")}
        self.dsem = {q: [nc.alloc_semaphore(name=f"ds_{q}{i}") for i in range(8)]
                     for q in ("Y", "G", "S")}
        self.dcnt = {q: [0] * 8 for q in ("Y", "G", "S")}
        self.didx = {q: 0 for q in ("Y", "G", "S")}
        self.wmax = {k: {} for k in self.ENG}
        self.writer = {}
        self.readers = {}

    def _wait(self, ek, toks):
        eng = self.eng[ek]
        best = {}
        for (sem, val) in toks:
            if val <= 0:
                continue
            k = sem.num
            if val > best.get(k, (None, 0))[1]:
                best[k] = (sem, val)
        for k, (sem, val) in sorted(best.items()):
            if self.wmax[ek].get(k, 0) >= val:
                continue
            eng.wait_ge(sem, val)
            self.wmax[ek][k] = val

    def _deps(self, reads, writes):
        toks = []
        for k in reads:
            t = self.writer.get(k)
            if t:
                toks.append(t)
        for k in writes:
            t = self.writer.get(k)
            if t:
                toks.append(t)
            toks.extend(self.readers.get(k, {}).values())
        return toks

    def _record(self, tok, ek, reads, writes):
        for k in writes:
            self.writer[k] = tok
            self.readers[k] = {}
        for k in reads:
            self.readers.setdefault(k, {})[ek] = tok

    def op(self, ek, emit, reads=(), writes=()):
        self._wait(ek, self._deps(reads, writes))
        inst = emit(self.eng[ek])
        self.cnt[ek] += 1
        inst.then_inc(self.sem[ek], 1)
        tok = (self.sem[ek], self.cnt[ek])
        self._record(tok, ek, reads, writes)
        return tok

    def group(self, ek, emits, reads=(), writes=()):
        self._wait(ek, self._deps(reads, writes))
        eng = self.eng[ek]
        inst = None
        for e in emits:
            inst = e(eng)
        self.cnt[ek] += 1
        inst.then_inc(self.sem[ek], 1)
        tok = (self.sem[ek], self.cnt[ek])
        self._record(tok, ek, reads, writes)
        return tok

    def dma(self, qk, out, in_, reads=(), writes=(), transpose=False):
        ek = {"Y": "y", "G": "g", "S": "s"}[qk]
        self._wait(ek, self._deps(reads, writes))
        i = self.didx[qk]
        self.didx[qk] = (i + 1) % 8
        sem = self.dsem[qk][i]
        inst = self.eng[ek].dma_start(out=out, in_=in_, transpose=transpose) \
            if transpose else self.eng[ek].dma_start(out=out, in_=in_)
        self.dcnt[qk][i] += 16
        inst.then_inc(sem, 16)
        tok = (sem, self.dcnt[qk][i])
        self._record(tok, ek, reads, writes)
        return tok

    def barrier(self):
        toks = [(self.sem[k], self.cnt[k]) for k in ("t", "v", "s", "g")]
        for q in ("Y", "G", "S"):
            for i in range(8):
                toks.append((self.dsem[q][i], self.dcnt[q][i]))
        for ek in self.ENG:
            self._wait(ek, toks)


def emit_quant_head(P, W, src_key, src_ap, scr, act_recip=False):
    nb = W // 16
    tag = scr["tag"]
    red, sh, rsh, s = scr["red"], scr["sh"], scr["rsh"], scr["s"]
    kred, ksh, krsh, ks = tag + "red", tag + "sh", tag + "rsh", tag + "s"
    src3 = src_ap.rearrange("p (nb w) -> p nb w", w=16)
    P.op("v", lambda v: v.tensor_reduce(red[:, 0:nb], src3, AX.X, ALU.max,
                                        apply_absolute_value=True),
         reads=[src_key], writes=[kred])
    P.op("v", lambda v: v.tensor_scalar(sh[:, 0:nb], red[:, 0:nb], 1.0 / 12.0,
                                        5e-13, ALU.mult, ALU.max),
         reads=[kred], writes=[ksh])
    if act_recip:
        P.op("s", lambda sc: sc.activation(rsh[:, 0:nb], sh[:, 0:nb], AF.Ln),
             reads=[ksh], writes=[krsh])
        P.op("s", lambda sc: sc.activation(rsh[:, 0:nb], rsh[:, 0:nb], AF.Exp,
                                           scale=-1.0),
             reads=[krsh], writes=[krsh])
    else:
        P.op("v", lambda v: v.reciprocal(rsh[:, 0:nb], sh[:, 0:nb]),
             reads=[ksh], writes=[krsh])
    rb = rsh[:, 0:nb].unsqueeze(2).broadcast_to([128, nb, 16])
    s3 = s[:, 0:W].rearrange("p (nb w) -> p nb w", w=16)
    P.op("g", lambda g: g.tensor_tensor(s3, src3, rb, ALU.mult),
         reads=[src_key, krsh], writes=[ks])


def emit_quant_tail(P, W, dst_key, dst_ap, scr):
    nb = W // 16
    tag = scr["tag"]
    sh, s, aa, n1, n2 = scr["sh"], scr["s"], scr["aa"], scr["n1"], scr["n2"]
    sg = scr["sg"]
    ksh, ks, ka, kn1, kn2, ksg = (tag + "sh", tag + "s", tag + "aa",
                                  tag + "n1", tag + "n2", tag + "sg")
    shb = sh[:, 0:nb].unsqueeze(2).broadcast_to([128, nb, 16])
    P.op("s", lambda sc: sc.activation(aa[:, 0:W], s[:, 0:W], AF.Abs),
         reads=[ks], writes=[ka])
    P.op("s", lambda sc: sc.activation(sg[:, 0:W], s[:, 0:W], AF.Sign),
         reads=[ks], writes=[ksg])
    P.op("v", lambda v: v.tensor_scalar(n1[:, 0:W], aa[:, 0:W], 0.5, None,
                                        ALU.is_gt),
         reads=[ka], writes=[kn1])
    for thr in (1.5, 2.5, 3.5):
        P.op("v", lambda v, thr=thr: v.scalar_tensor_tensor(
            n1[:, 0:W], aa[:, 0:W], thr, n1[:, 0:W], ALU.is_gt, ALU.add),
            reads=[ka, kn1], writes=[kn1])
    P.op("v", lambda v: v.tensor_scalar(n2[:, 0:W], aa[:, 0:W], 5.0, None,
                                        ALU.is_gt),
         reads=[ka], writes=[kn2])
    for thr in (7.0, 10.0, 10.0):
        P.op("v", lambda v, thr=thr: v.scalar_tensor_tensor(
            n2[:, 0:W], aa[:, 0:W], thr, n2[:, 0:W], ALU.is_gt, ALU.add),
            reads=[ka, kn2], writes=[kn2])
    P.op("v", lambda v: v.scalar_tensor_tensor(
        n1[:, 0:W], n2[:, 0:W], 2.0, n1[:, 0:W], ALU.mult, ALU.add),
        reads=[kn1, kn2], writes=[kn1])
    P.op("v", lambda v: v.tensor_tensor(n1[:, 0:W], n1[:, 0:W], sg[:, 0:W],
                                        ALU.mult),
         reads=[kn1, ksg], writes=[kn1])
    qs3 = n1[:, 0:W].rearrange("p (nb w) -> p nb w", w=16)
    d3 = dst_ap.rearrange("p (nb w) -> p nb w", w=16)
    P.op("g", lambda g: g.tensor_tensor(d3, qs3, shb, ALU.mult),
         reads=[kn1, ksh], writes=[dst_key])


def emit_quant(P, W, src_key, src_ap, dst_key, dst_ap, scr, act_recip=False):
    emit_quant_head(P, W, src_key, src_ap, scr, act_recip=act_recip)
    emit_quant_tail(P, W, dst_key, dst_ap, scr)


def quant_scratch(alloc, pre, W):
    return dict(tag=pre,
                red=alloc(pre + "red", [128, W // 16], F32),
                sh=alloc(pre + "sh", [128, W // 16], F32),
                rsh=alloc(pre + "rsh", [128, W // 16], F32),
                s=alloc(pre + "s", [128, W], F32),
                aa=alloc(pre + "aa", [128, W], F32),
                n1=alloc(pre + "n1", [128, W], F32),
                n2=alloc(pre + "n2", [128, W], F32),
                sg=alloc(pre + "sg", [128, W], BF16))


def quant_scratch2(alloc, pre, W):
    return [quant_scratch(alloc, pre + str(i), W) for i in range(2)]


def build_program(dbg=frozenset(), phases=("A", "W", "B", "C", "D", "E", "F")):
    nc = bass.Bass("TRN2", target_bir_lowering=False)
    P = PR(nc)

    x = nc.dram_tensor("x", [T, C], F32, kind="ExternalInput").ap()
    qkv_w = nc.dram_tensor("qkv_w", [3 * C, C], F32, kind="ExternalInput").ap()
    proj_w = nc.dram_tensor("proj_w", [C, C], F32, kind="ExternalInput").ap()
    wg = nc.dram_tensor("wg", [I, C], F32, kind="ExternalInput").ap()
    wu = nc.dram_tensor("wu", [I, C], F32, kind="ExternalInput").ap()
    wd = nc.dram_tensor("wd", [C, I], F32, kind="ExternalInput").ap()
    qkvb = nc.dram_tensor("qkvb", [1, 3 * C], F32, kind="ExternalInput").ap()
    projb = nc.dram_tensor("projb", [1, C], F32, kind="ExternalInput").ap()
    costm = nc.dram_tensor("costm", [T, 32], F32, kind="ExternalInput").ap()
    sintm = nc.dram_tensor("sintm", [T, 32], F32, kind="ExternalInput").ap()
    iden = nc.dram_tensor("iden", [128, 128], F32, kind="ExternalInput").ap()
    rowind = nc.dram_tensor("rowind", [80, H * 128], F32, kind="ExternalInput").ap()
    out = nc.dram_tensor("out", [T, C], F32, kind="ExternalOutput").ap()

    xn_bf = nc.dram_tensor("xn_bf", [T, C], BF16, kind="Internal").ap()
    xq_bf = nc.dram_tensor("xq_bf", [T, C], BF16, kind="Internal").ap()
    qkvw_bf = nc.dram_tensor("qkvw_bf", [3 * C, C], BF16, kind="Internal").ap()
    projw_bf = nc.dram_tensor("projw_bf", [C, C], BF16, kind="Internal").ap()
    wgq_bf = nc.dram_tensor("wgq_bf", [I, C], BF16, kind="Internal").ap()
    wuq_bf = nc.dram_tensor("wuq_bf", [I, C], BF16, kind="Internal").ap()
    wdq_bf = nc.dram_tensor("wdq_bf", [C, I], BF16, kind="Internal").ap()
    q_bfd = nc.dram_tensor("q_bfd", [T, C], BF16, kind="Internal").ap()
    k_bfd = nc.dram_tensor("k_bfd", [T, C], BF16, kind="Internal").ap()
    hq_bf = nc.dram_tensor("hq_bf", [T, I], BF16, kind="Internal").ap()

    den_d = [nc.dram_tensor(f"den_d{i}", [1, 512], F32, kind="Internal").ap()
             for i in range(2)]
    dbg_outs = {}

    def dbg_dump(name, src_ap, after_keys):
        if name not in dbg:
            return
        o = nc.dram_tensor("dbg_" + name, list(src_ap.shape), src_ap.dtype,
                           kind="ExternalOutput").ap()
        P.dma("Y", o, src_ap, reads=list(after_keys), writes=["dbg_" + name])
        dbg_outs[name] = "dbg_" + name

    ctx = ExitStack()

    def SB(st, name, shape, dt, **kw):
        return st.enter_context(nc.sbuf_tensor(name, shape, dt, **kw))

    def PS(st, name, shape):
        return st.enter_context(nc.psum_tensor(name, shape, F32))

    with nc.Block():
        identsb = SB(ctx, "identsb", [128, 128], F32)
        projb_b = SB(ctx, "projb_b", [128, C], F32)
        ones164 = SB(ctx, "ones164", [1, 64], F32)
        P.dma("Y", identsb[:, :], iden, writes=["identsb"])
        P.dma("Y", projb_b[:, :], projb.broadcast_to([128, C]), writes=["projb_b"])
        P.op("v", lambda v: v.memset(ones164[:, :], 1.0), writes=["ones164"])
        P.dma("G", qkvw_bf, qkv_w, writes=["qkvw_bf"])
        P.dma("G", projw_bf, proj_w, writes=["projw_bf"])

        if "A" in phases:
            with ExitStack() as st:
                al = lambda n, sh, dt: SB(st, n, sh, dt)
                xt = [al(f"xa{i}", [128, C], F32) for i in range(2)]
                xn32 = [al(f"xn32{i}", [128, C], F32) for i in range(2)]
                xnb = al("xnb", [128, C], BF16)
                qb = [al(f"qb{i}", [128, C], BF16) for i in range(2)]
                sx = [al(f"sx_{i}", [128, 1], F32) for i in range(2)]
                sx2 = [al(f"sx2_{i}", [128, 1], F32) for i in range(2)]
                mvar = [al(f"mvar{i}", [128, 4], F32) for i in range(2)]
                mb = [al(f"mb{i}", [128, 1], F32) for i in range(2)]
                scrs = quant_scratch2(al, "qA", C)
                for n in range(NT):
                    b = n % 2
                    kx = f"xa{b}"
                    P.dma("Y", xt[b][:, :], x[128 * n:128 * n + 128, :], writes=[kx])
                    P.op("v", lambda v, b=b: v.tensor_reduce(
                        sx[b][:, :], xt[b][:, :], AX.X, ALU.add),
                        reads=[kx], writes=[f"sx{b}"])
                    P.op("s", lambda sc, b=b: sc.activation(
                        scrs[b]["aa"][:, :], xt[b][:, :], AF.Square,
                        accum_out=sx2[b][:, :]),
                        reads=[kx], writes=[f"qA{b}aa", f"sx2{b}"])
                    P.op("v", lambda v: v.tensor_scalar(
                        mvar[b][:, 0:1], sx[b][:, :], 1.0 / C, None, ALU.mult),
                        reads=[f"sx{b}"], writes=[f"m{b}"])
                    P.op("v", lambda v: v.tensor_scalar(
                        mvar[b][:, 1:2], sx2[b][:, :], 1.0 / C, None, ALU.mult),
                        reads=[f"sx2{b}"], writes=[f"ex2{b}"])
                    P.op("s", lambda sc: sc.activation(
                        mvar[b][:, 3:4], mvar[b][:, 0:1], AF.Square),
                        reads=[f"m{b}"], writes=[f"m2{b}"])
                    P.op("v", lambda v: v.scalar_tensor_tensor(
                        mvar[b][:, 2:3], mvar[b][:, 1:2], 1e-5, mvar[b][:, 3:4],
                        ALU.add, ALU.subtract),
                        reads=[f"ex2{b}", f"m2{b}"], writes=[f"var{b}"])
                    P.op("s", lambda sc: sc.activation(
                        mvar[b][:, 2:3], mvar[b][:, 2:3], AF.Sqrt),
                        reads=[f"var{b}"], writes=[f"var{b}"])
                    P.op("v", lambda v: v.reciprocal(mvar[b][:, 3:4], mvar[b][:, 2:3]),
                        reads=[f"var{b}", f"m2{b}"], writes=[f"rstd{b}"])
                    P.op("v", lambda v: v.scalar_tensor_tensor(
                        mb[b][:, :], mvar[b][:, 0:1], -1.0, mvar[b][:, 3:4],
                        ALU.mult, ALU.mult),
                        reads=[f"m{b}", f"rstd{b}"], writes=[f"mb{b}"])
                    P.op("s", lambda sc, b=b: sc.activation(
                        xn32[b][:, :], xt[b][:, :], AF.Identity,
                        bias=mb[b][:, :], scale=mvar[b][:, 3:4]),
                        reads=[kx, f"mb{b}", f"rstd{b}"], writes=[f"xn32{b}"])
                    P.op("s", lambda sc, b=b: sc.activation(
                        xnb[:, :], xt[b][:, :], AF.Identity,
                        bias=mb[b][:, :], scale=mvar[b][:, 3:4]),
                        reads=[kx, f"mb{b}", f"rstd{b}"], writes=["xnb"])
                    P.dma("G", xn_bf[128 * n:128 * n + 128, :], xnb[:, :],
                          reads=["xnb"], writes=[f"xn_bf{n}"])
                    emit_quant_head(P, C, f"xn32{b}", xn32[b][:, :], scrs[b])
                    if n > 0:
                        bp = (n - 1) % 2
                        emit_quant_tail(P, C, f"qb{bp}", qb[bp][:, :], scrs[bp])
                        P.dma("G", xq_bf[128 * (n - 1):128 * n, :],
                              qb[bp][:, :], reads=[f"qb{bp}"],
                              writes=[f"xq_bf{n - 1}"])
                bp = (NT - 1) % 2
                emit_quant_tail(P, C, f"qb{bp}", qb[bp][:, :], scrs[bp])
                P.dma("G", xq_bf[128 * (NT - 1):128 * NT, :], qb[bp][:, :],
                      reads=[f"qb{bp}"], writes=[f"xq_bf{NT - 1}"])
                P.barrier()
            dbg_dump("xn_bf", xn_bf, [f"xn_bf{n}" for n in range(NT)])
            dbg_dump("xq_bf", xq_bf, [f"xq_bf{n}" for n in range(NT)])

        stQ = ExitStack()
        stKV = ExitStack()
        q_fm = [SB(stQ, f"qfm{c}", [128, T], BF16) for c in range(CCH)]
        k_fm = [SB(stKV, f"kfm{c}", [128, T], BF16) for c in range(CCH)]
        vone = [SB(stKV, f"vone{n}", [128, H * 65], BF16) for n in range(NT)]
        NM = SB(stKV, "NM", [80, T], BF16)
        rowbf = SB(stKV, "rowbf", [80, H * 128], BF16)

        wjobs = []
        for r in range(I // 128):
            wjobs.append((wg, wgq_bf, r, 0, C, f"wgq{r}"))
            wjobs.append((wu, wuq_bf, r, 0, C, f"wuq{r}"))
        for r in range(C // 128):
            for j, (c0, cw) in enumerate([(0, 1024), (1024, 1024), (2048, 768)]):
                wjobs.append((wd, wdq_bf, r, c0, cw, f"wdq{r}_{j}"))
        WGK = [f"wgq{r}" for r in range(I // 128)]
        WUK = [f"wuq{r}" for r in range(I // 128)]
        WDK = [f"wdq{r}_{j}" for r in range(C // 128) for j in range(3)]

        if "B" in phases:
            with ExitStack() as st:
                al = lambda n, sh, dt: SB(st, n, sh, dt)
                xn_fm = [al(f"xnfm{c}", [128, T], BF16) for c in range(CCH)]
                wstrip = [al(f"wstr{c}", [128, 512], BF16) for c in range(CCH)]
                bstrip = al("bstrip", [128, 512], F32)
                cost_a = al("cost_a", [128, 512], F32)
                sint_a = al("sint_a", [128, 512], F32)
                qk32 = [al(f"qk32_{i}", [128, 512], F32) for i in range(2)]
                t1 = [al(f"t1_{i}", [128, 256], F32) for i in range(2)]
                t2 = [al(f"t2_{i}", [128, 256], F32) for i in range(2)]
                t3 = [al(f"t3_{i}", [128, 256], F32) for i in range(2)]
                t4 = [al(f"t4_{i}", [128, 256], F32) for i in range(2)]
                qtmb = [al(f"qtmb{i}", [128, 512], BF16) for i in range(2)]
                qsq = [al(f"qsq{i}", [128, 512], F32) for i in range(2)]
                nrm = [al(f"nrm{i}", [128, 8], F32) for i in range(2)]
                qnorm_all = al("qnorm_all", [128, H * NT], F32)
                kmax_acc = al("kmax_acc", [128, H], F32)
                kmx = al("kmx", [16, 1], F32)
                kng = al("kng", [80, 1], F32)
                c560 = al("c560", [80, 1], F32)
                PSqk = [PS(st, f"PSqk{i}", [128, 512]) for i in range(4)]
                PSt = PS(st, "PSt", [128, 128])

                P.dma("Y", cost_a[:, :].rearrange("p (n j) -> p n j", j=32),
                      costm.rearrange("(n p) j -> p n j", p=128),
                      writes=["costall"])
                P.dma("Y", sint_a[:, :].rearrange("p (n j) -> p n j", j=32),
                      sintm.rearrange("(n p) j -> p n j", p=128),
                      writes=["sintall"])
                for n in range(NT):
                    P.op("g", lambda g, n=n: g.memset(vone[n][:, :], 1.0),
                         writes=[f"vone{n}"])
                P.op("v", lambda v: v.memset(kmax_acc[:, :], 0.0), writes=["kmacc"])
                P.op("v", lambda v: v.memset(c560[:, :], 560.0), writes=["c560"])
                P.dma("G", rowbf[:, :], rowind, writes=["rowbf"])
                for c in range(CCH):
                    P.dma("Y", xn_fm[c][:, :], xn_bf[:, 128 * c:128 * c + 128],
                          reads=[f"xn_bf{n}" for n in range(NT)],
                          writes=[f"xnfm{c}"], transpose=True)

                for fs in range(6):
                    for c in range(CCH):
                        P.dma("Y", wstrip[c][:, :],
                              qkvw_bf[512 * fs:512 * fs + 512,
                                      128 * c:128 * c + 128],
                              reads=["qkvw_bf"], writes=[f"wstr{c}"],
                              transpose=True)
                    P.dma("G", bstrip[:, :],
                          qkvb[:, 512 * fs:512 * fs + 512].broadcast_to([128, 512]),
                          writes=["bstrip"])
                    for n in range(NT):
                        b = n % 2
                        bp = n % 4
                        kps = f"PSqk{bp}"
                        P.group("t", [
                            (lambda t, c=c, bp=bp, n=n,
                             first=(c == 0), last=(c == CCH - 1): t.matmul(
                                PSqk[bp][:, :], xn_fm[c][:, 128 * n:128 * n + 128],
                                wstrip[c][:, :], start=first, stop=last))
                            for c in range(CCH)],
                            reads=[f"xnfm{c}" for c in range(CCH)] +
                                  [f"wstr{c}" for c in range(CCH)],
                            writes=[kps])
                        if fs < 4:
                            P.op("v", lambda v, b=b, bp=bp: v.tensor_tensor(
                                qk32[b][:, :], PSqk[bp][:, :], bstrip[:, :], ALU.add),
                                reads=[kps, "bstrip"], writes=[f"qk32{b}"])
                            q3 = qk32[b][:, :].rearrange("p (h d) -> p h d", h=8)
                            cb = cost_a[:, 32 * n:32 * n + 32].unsqueeze(
                                1).broadcast_to([128, 8, 32])
                            sbb = sint_a[:, 32 * n:32 * n + 32].unsqueeze(
                                1).broadcast_to([128, 8, 32])
                            t13 = t1[b][:, :].rearrange("p (h d) -> p h d", h=8)
                            t23 = t2[b][:, :].rearrange("p (h d) -> p h d", h=8)
                            t33 = t3[b][:, :].rearrange("p (h d) -> p h d", h=8)
                            t43 = t4[b][:, :].rearrange("p (h d) -> p h d", h=8)
                            o3 = qtmb[b][:, :].rearrange("p (h d) -> p h d", h=8)
                            P.op("v", lambda v, q3=q3, cb=cb, t13=t13:
                                 v.tensor_tensor(t13, q3[:, :, 0:32], cb, ALU.mult),
                                 reads=[f"qk32{b}", "costall"], writes=[f"t1{b}"])
                            P.op("v", lambda v, q3=q3, sbb=sbb, t23=t23:
                                 v.tensor_tensor(t23, q3[:, :, 32:64], sbb, ALU.mult),
                                 reads=[f"qk32{b}", "sintall"], writes=[f"t2{b}"])
                            P.op("v", lambda v, q3=q3, sbb=sbb, t33=t33:
                                 v.tensor_tensor(t33, q3[:, :, 0:32], sbb, ALU.mult),
                                 reads=[f"qk32{b}", "sintall"], writes=[f"t3{b}"])
                            P.op("v", lambda v, q3=q3, cb=cb, t43=t43:
                                 v.tensor_tensor(t43, q3[:, :, 32:64], cb, ALU.mult),
                                 reads=[f"qk32{b}", "costall"], writes=[f"t4{b}"])
                            P.op("g", lambda g, o3=o3, t13=t13, t23=t23:
                                 g.tensor_tensor(o3[:, :, 0:32], t13, t23,
                                                 ALU.subtract),
                                 reads=[f"t1{b}", f"t2{b}"], writes=[f"qtmb{b}"])
                            P.op("g", lambda g, o3=o3, t33=t33, t43=t43:
                                 g.tensor_tensor(o3[:, :, 32:64], t33, t43, ALU.add),
                                 reads=[f"t3{b}", f"t4{b}"], writes=[f"qtmb{b}"])
                            dstd = q_bfd if fs < 2 else k_bfd
                            fo = 512 * fs if fs < 2 else 512 * (fs - 2)
                            kname = ("q_bfd" if fs < 2 else "k_bfd") + f"{fs}_{n}"
                            P.dma("G", dstd[128 * n:128 * n + 128, fo:fo + 512],
                                  qtmb[b][:, :], reads=[f"qtmb{b}"], writes=[kname])
                            P.op("s", lambda sc, b=b: sc.activation(
                                qsq[b][:, :], qtmb[b][:, :], AF.Square),
                                reads=[f"qtmb{b}"], writes=[f"qsq{b}"])
                            qs3 = qsq[b][:, :].rearrange("p (h d) -> p h d", h=8)
                            P.op("v", lambda v, qs3=qs3, b=b: v.tensor_reduce(
                                nrm[b][:, :], qs3, AX.X, ALU.add),
                                reads=[f"qsq{b}"], writes=[f"nrm{b}"])
                            P.op("s", lambda sc, b=b: sc.activation(
                                nrm[b][:, :], nrm[b][:, :], AF.Sqrt),
                                reads=[f"nrm{b}"], writes=[f"nrm{b}"])
                            if fs < 2:
                                co = H * n + 8 * fs
                                P.op("v", lambda v, co=co, b=b: v.tensor_copy(
                                    qnorm_all[:, co:co + 8], nrm[b][:, :]),
                                    reads=[f"nrm{b}"], writes=["qna"])
                            else:
                                co = 8 * (fs - 2)
                                P.op("v", lambda v, co=co, b=b: v.tensor_tensor(
                                    kmax_acc[:, co:co + 8], kmax_acc[:, co:co + 8],
                                    nrm[b][:, :], ALU.max),
                                    reads=[f"nrm{b}", "kmacc"], writes=["kmacc"])
                        else:
                            h0 = 8 * (fs - 4)
                            pv3 = PSqk[bp][:, :].rearrange("p (h d) -> p h d", h=8)
                            bb3 = bstrip[:, :].rearrange("p (h d) -> p h d", h=8)
                            dst3 = vone[n][:, 65 * h0:65 * h0 + 520].rearrange(
                                "p (h e) -> p h e", e=65)[:, :, 0:64]
                            P.op("v", lambda v, pv3=pv3, bb3=bb3, dst3=dst3:
                                 v.tensor_tensor(dst3, pv3, bb3, ALU.add),
                                 reads=[kps, "bstrip", f"vone{n}"],
                                 writes=[f"vone{n}"])
                    if fs == 1:
                        for c in range(CCH):
                            P.dma("Y", q_fm[c][:, :],
                                  q_bfd[:, 128 * c:128 * c + 128],
                                  reads=[f"q_bfd{f}_{n}" for f in range(2)
                                         for n in range(NT)],
                                  writes=[f"qfm{c}"], transpose=True)
                    if fs == 3:
                        for c in range(CCH):
                            P.dma("Y", k_fm[c][:, :],
                                  k_bfd[:, 128 * c:128 * c + 128],
                                  reads=[f"k_bfd{f}_{n}" for f in range(2, 4)
                                         for n in range(NT)],
                                  writes=[f"kfm{c}"], transpose=True)
                        P.group("t", [lambda t: t.transpose(
                            PSt[0:16, :], kmax_acc[:, :], identsb[:, :])],
                            reads=["kmacc", "identsb"], writes=["PSt"])
                        P.op("v", lambda v: v.tensor_reduce(
                            kmx[:, :], PSt[0:16, :], AX.X, ALU.max),
                            reads=["PSt"], writes=["kmx"])
                        P.op("v", lambda v: v.tensor_scalar(
                            kng[0:16, :], kmx[:, :], -1.0, None, ALU.mult),
                            reads=["kmx"], writes=["kng"])
                        P.op("v", lambda v: v.tensor_copy(
                            kng[64:80, :], kng[0:16, :]),
                            reads=["kng"], writes=["kng"])
                        for n in range(NT):
                            P.group("t", [lambda t, n=n: t.transpose(
                                PSt[0:16, :], qnorm_all[:, H * n:H * n + H],
                                identsb[:, :])],
                                reads=["qna", "identsb"], writes=["PSt"])
                            P.op("s", lambda sc, n=n: sc.activation(
                                NM[0:16, 128 * n:128 * n + 128], PSt[0:16, :],
                                AF.Identity, bias=c560[0:16, :],
                                scale=kng[0:16, :]),
                                reads=["PSt", "kng", "c560"], writes=["NM"])
                            P.op("v", lambda v, n=n: v.tensor_copy(
                                NM[64:80, 128 * n:128 * n + 128],
                                NM[0:16, 128 * n:128 * n + 128]),
                                reads=["NM"], writes=["NM"])
                if "qna" in dbg:
                    qo = nc.dram_tensor("dbg_qna", [128, H * NT], F32,
                                        kind="ExternalOutput").ap()
                    P.dma("Y", qo, qnorm_all[:, :], reads=["qna"],
                          writes=["dbg_qna"])
                    dbg_outs["qna"] = "dbg_qna"
                P.barrier()
            dbg_dump("q_bfd", q_bfd,
                     [f"q_bfd{fs}_{n}" for fs in range(2) for n in range(NT)])
            dbg_dump("k_bfd", k_bfd,
                     [f"k_bfd{fs}_{n}" for fs in range(2, 4) for n in range(NT)])

        if "C" in phases:
            with ExitStack() as st:
                al = lambda n, sh, dt: SB(st, n, sh, dt)
                pT = [al(f"pT{i}", [128, 1024], BF16) for i in range(2)]
                rrow = [al(f"rrow{i}", [1, 512], F32) for i in range(2)]
                rb = [al(f"rb{i}", [64, 512], F32) for i in range(2)]
                wt_w = [al(f"wt{i}", [128, 1024], F32) for i in range(4)]
                wqb_w = [al(f"wqb{i}", [128, 1024], BF16) for i in range(2)]
                wscrs = quant_scratch2(al, "qW", 1024)
                PSs = [PS(st, f"PSs{i}", [128, 1024]) for i in range(2)]
                PSp = [[PS(st, f"PSpa{i}", [128, 512]),
                        PS(st, f"PSpb{i}", [128, 512])] for i in range(2)]
                _wj = {"l": 0, "h": 0, "t": 0}

                def _wload(i):
                    src, dst, r, c0, cw, key = wjobs[i]
                    b = i % 4
                    P.dma("G", wt_w[b][:, 0:cw],
                          src[128 * r:128 * r + 128, c0:c0 + cw],
                          writes=[f"wt{b}"])

                def _whead(i):
                    src, dst, r, c0, cw, key = wjobs[i]
                    b = i % 4
                    emit_quant_head(P, cw, f"wt{b}", wt_w[b][:, 0:cw],
                                    wscrs[i % 2])

                def _wtail(i):
                    src, dst, r, c0, cw, key = wjobs[i]
                    b = i % 2
                    emit_quant_tail(P, cw, f"wqb{b}", wqb_w[b][:, 0:cw],
                                    wscrs[b])
                    P.dma("G", dst[128 * r:128 * r + 128, c0:c0 + cw],
                          wqb_w[b][:, 0:cw], reads=[f"wqb{b}"], writes=[key])

                def emit_wjobs(k):
                    while k > 0 and _wj["t"] < len(wjobs):
                        while _wj["l"] < min(len(wjobs), _wj["t"] + 4):
                            _wload(_wj["l"])
                            _wj["l"] += 1
                        while _wj["h"] < min(len(wjobs), _wj["t"] + 2):
                            _whead(_wj["h"])
                            _wj["h"] += 1
                        _wtail(_wj["t"])
                        _wj["t"] += 1
                        k -= 1

                while _wj["l"] < 4:
                    _wload(_wj["l"])
                    _wj["l"] += 1

                def emit_S(hp, qb, kt):
                    b = kt % 2
                    P.group("t", [
                        (lambda t: t.matmul(
                            PSs[b][:, 0:512],
                            k_fm[hp][0:64, 128 * kt:128 * kt + 128],
                            q_fm[hp][0:64, 512 * qb:512 * qb + 512],
                            start=True, stop=False, tile_position=(0, 0))),
                        (lambda t: t.matmul(
                            PSs[b][:, 512:1024],
                            k_fm[hp][64:128, 128 * kt:128 * kt + 128],
                            q_fm[hp][64:128, 512 * qb:512 * qb + 512],
                            start=True, stop=False, tile_position=(64, 0))),
                        (lambda t: t.matmul(
                            PSs[b][:, 0:512],
                            rowbf[0:16, 128 * (2 * hp):128 * (2 * hp) + 128],
                            NM[0:16, 512 * qb:512 * qb + 512],
                            start=False, stop=True, tile_position=(0, 0))),
                        (lambda t: t.matmul(
                            PSs[b][:, 512:1024],
                            rowbf[64:80,
                                  128 * (2 * hp + 1):128 * (2 * hp + 1) + 128],
                            NM[64:80, 512 * qb:512 * qb + 512],
                            start=False, stop=True, tile_position=(64, 0)))],
                        reads=[f"qfm{hp}", f"kfm{hp}", "NM", "rowbf",
                               f"qfm{hp}_a_{qb}", f"qfm{hp}_b_{qb}"],
                        writes=[f"PSs{b}"])
                    P.op("s", lambda sc: sc.activation(
                        pT[b][:, :], PSs[b][:, :], AF.Exp, scale=0.125),
                        reads=[f"PSs{b}"], writes=[f"pT{b}"])

                def emit_PV(hp, kt, j):
                    b = kt % 2
                    pa, pb = PSp[j % 2]
                    P.group("t", [
                        (lambda t: t.matmul(
                            pa[0:65, :],
                            vone[kt][:, 130 * hp:130 * hp + 65],
                            pT[b][:, 0:512],
                            start=(kt == 0), stop=(kt == NT - 1))),
                        (lambda t: t.matmul(
                            pb[0:65, :],
                            vone[kt][:, 130 * hp + 65:130 * hp + 130],
                            pT[b][:, 512:1024],
                            start=(kt == 0), stop=(kt == NT - 1)))],
                        reads=[f"pT{b}", f"vone{kt}"],
                        writes=[f"PSpa{j % 2}", f"PSpb{j % 2}"])

                def emit_evac_recips(hp, qb, j):
                    for hh in range(2):
                        Pp, kP = PSp[j % 2][hh], f"PSp{'ab'[hh]}{j % 2}"
                        P.op("v", lambda v, Pp=Pp, hh=hh: v.reciprocal(
                            rrow[hh][:, :], Pp[64:65, :]),
                            reads=[kP], writes=[f"rrow{hh}"])
                        P.dma("G", den_d[hh], rrow[hh][:, :],
                              reads=[f"rrow{hh}"], writes=[f"den_d{hh}"])
                        P.dma("Y", rb[hh][:, :],
                              den_d[hh].broadcast_to([64, 512]),
                              reads=[f"den_d{hh}"], writes=[f"rb{hh}"])

                def emit_evac_mults(hp, qb, j):
                    for hh in range(2):
                        Pp, kP = PSp[j % 2][hh], f"PSp{'ab'[hh]}{j % 2}"
                        ro = 64 * hh
                        P.op("v", lambda v, Pp=Pp, hp=hp, qb=qb, ro=ro, hh=hh:
                             v.tensor_tensor(
                                 q_fm[hp][ro:ro + 64, 512 * qb:512 * qb + 512],
                                 Pp[0:64, :], rb[hh][:, :], ALU.mult),
                             reads=[kP, f"rb{hh}"],
                             writes=[f"qfm{hp}_{'ab'[hh]}_{qb}"])

                blocks = [(hp, qb) for hp in range(CCH) for qb in range(4)]
                nbk = len(blocks)
                nwj = len(wjobs)
                for j, (hp, qb) in enumerate(blocks):
                    quota = (nwj * (j + 1)) // nbk - (nwj * j) // nbk
                    for kt in range(NT):
                        emit_S(hp, qb, kt)
                        if kt == 0 and j > 0:
                            emit_evac_recips(*blocks[j - 1], j - 1)
                        if kt > 0:
                            emit_PV(hp, kt - 1, j)
                        if kt == 1 and quota >= 1:
                            emit_wjobs(1)
                        if kt == 4 and j > 0:
                            emit_evac_mults(*blocks[j - 1], j - 1)
                        if kt == 7 and quota >= 2:
                            emit_wjobs(1)
                        if kt == 12 and quota >= 3:
                            emit_wjobs(quota - 2)
                    emit_PV(hp, NT - 1, j)
                emit_evac_recips(*blocks[-1], len(blocks) - 1)
                emit_evac_mults(*blocks[-1], len(blocks) - 1)
                emit_wjobs(len(wjobs))
                P.barrier()
            if "ao_d" in dbg:
                ao_d = nc.dram_tensor("dbg_ao_d", [C, T], BF16,
                                      kind="ExternalOutput").ap()
                for c in range(CCH):
                    P.dma("Y", ao_d[128 * c:128 * c + 128, :], q_fm[c][:, :],
                          reads=[f"qfm{c}_{ab}_{qb}" for ab in "ab"
                                 for qb in range(4)],
                          writes=["dbg_ao_d"])
                dbg_outs["ao_d"] = "dbg_ao_d"
            if "negm" in dbg:
                nm_d = nc.dram_tensor("dbg_negm", [80, T], BF16,
                                      kind="ExternalOutput").ap()
                P.dma("Y", nm_d, NM[:, :], reads=["NM"], writes=["dbg_negm"])
                dbg_outs["negm"] = "dbg_negm"
        stKV.close()

        acc = [SB(ctx, f"acc{n}", [128, C], F32, side="right") for n in range(NT)]
        if "D" in phases:
            with ExitStack() as st:
                pwT = [SB(st, f"pwT{c}", [128, C], BF16) for c in range(CCH)]
                PSpj = [PS(st, f"PSpj{i}", [128, 512]) for i in range(2)]
                for c in range(CCH):
                    P.dma("S", pwT[c][:, :], projw_bf[:, 128 * c:128 * c + 128],
                          reads=["projw_bf"], writes=[f"pwT{c}"], transpose=True)
                aokeys = [f"qfm{c}_{ab}_{qb}" for c in range(CCH)
                          for ab in "ab" for qb in range(4)]
                for n in range(NT):
                    for jh in range(2):
                        b = jh
                        kps = f"PSpj{b}"
                        P.group("t", [
                            (lambda t, c=c, n=n, jh=jh, b=b,
                             first=(c == 0), last=(c == CCH - 1): t.matmul(
                                PSpj[b][:, :], q_fm[c][:, 128 * n:128 * n + 128],
                                pwT[c][:, 512 * jh:512 * jh + 512],
                                start=first, stop=last))
                            for c in range(CCH)],
                            reads=aokeys + [f"pwT{c}" for c in range(CCH)],
                            writes=[kps])
                        P.op("v", lambda v, n=n, jh=jh, b=b: v.tensor_tensor(
                            acc[n][:, 512 * jh:512 * jh + 512], PSpj[b][:, :],
                            projb_b[:, 512 * jh:512 * jh + 512], ALU.add),
                            reads=[kps, "projb_b"], writes=[f"acc{n}"])
                P.barrier()
            if "acc_d" in dbg:
                acc_d = nc.dram_tensor("dbg_acc_d", [T, C], F32,
                                       kind="ExternalOutput").ap()
                for n in range(NT):
                    P.dma("Y", acc_d[128 * n:128 * n + 128, :], acc[n][:, :],
                          reads=[f"acc{n}"], writes=["dbg_acc_d"])
                dbg_outs["acc_d"] = "dbg_acc_d"
        stQ.close()

        if "E" in phases:
            with ExitStack() as st:
                al = lambda n, sh, dt: SB(st, n, sh, dt)
                xq_fm = [al(f"xqfm{c}", [128, T], BF16) for c in range(CCH)]
                wgT = [[al(f"wgT{c}_{i}", [128, 512], BF16)
                        for c in range(CCH)] for i in range(2)]
                wuT = [[al(f"wuT{c}_{i}", [128, 512], BF16)
                        for c in range(CCH)] for i in range(2)]
                wdT = [al(f"wdT{m}", [128, C], BF16) for m in range(4)]
                hq_fm = [al(f"hqfm{m}", [128, T], BF16) for m in range(4)]
                sil = [al(f"sil{i}", [128, 512], F32) for i in range(2)]
                h32 = [al(f"h32{i}", [128, 1024], F32) for i in range(2)]
                scrs = quant_scratch2(al, "qE", 1024)
                hqb = [scrs[i]["sg"] for i in range(2)]
                xf = [scrs[0]["s"], scrs[1]["s"]]
                ot = [scrs[0]["aa"], scrs[1]["aa"]]
                PSg = [PS(st, f"PSg{i}", [128, 512]) for i in range(2)]
                PSu = [PS(st, f"PSu{i}", [128, 512]) for i in range(2)]
                PSd = [PS(st, f"PSd{i}", [128, C]) for i in range(2)]
                for c in range(CCH):
                    P.dma("Y", xq_fm[c][:, :], xq_bf[:, 128 * c:128 * c + 128],
                          reads=[f"xq_bf{n}" for n in range(NT)],
                          writes=[f"xqfm{c}"], transpose=True)

                def emit_gup(ib, n):
                    i0, iw = IBLKS[ib]
                    b = n % 2
                    wp = ib % 2
                    P.group("t", [
                        (lambda t, c=c, n=n, iw=iw, i0=i0, b=b, wp=wp,
                         first=(c == 0), last=(c == CCH - 1): t.matmul(
                            PSg[b][:, 0:iw], xq_fm[c][:, 128 * n:128 * n + 128],
                            wgT[wp][c][:, 0:iw], start=first, stop=last))
                        for c in range(CCH)],
                        reads=[f"xqfm{c}" for c in range(CCH)] +
                              [f"wgT{c}_{wp}" for c in range(CCH)],
                        writes=[f"PSg{b}"])
                    P.group("t", [
                        (lambda t, c=c, n=n, iw=iw, i0=i0, b=b, wp=wp,
                         first=(c == 0), last=(c == CCH - 1): t.matmul(
                            PSu[b][:, 0:iw], xq_fm[c][:, 128 * n:128 * n + 128],
                            wuT[wp][c][:, 0:iw], start=first, stop=last))
                        for c in range(CCH)],
                        reads=[f"xqfm{c}" for c in range(CCH)] +
                              [f"wuT{c}_{wp}" for c in range(CCH)],
                        writes=[f"PSu{b}"])
                    pp = (n // 2) % 2
                    ho = iw * (n % 2)
                    P.op("s", lambda sc, iw=iw, b=b: sc.activation(
                        sil[b][:, 0:iw], PSg[b][:, 0:iw], AF.Silu),
                        reads=[f"PSg{b}"], writes=[f"sil{b}"])
                    P.op("v", lambda v, iw=iw, b=b, pp=pp, ho=ho: v.tensor_tensor(
                        h32[pp][:, ho:ho + iw], sil[b][:, 0:iw],
                        PSu[b][:, 0:iw], ALU.mult),
                        reads=[f"sil{b}", f"PSu{b}"], writes=[f"h32{pp}"])

                def emit_down(ib, n):
                    i0, iw = IBLKS[ib]
                    nm = iw // 128
                    b = n % 2
                    kps = f"PSd{b}"
                    P.group("t", [
                        (lambda t, m=m, jh=jh, n=n, b=b,
                         stt=(m == 0), sp=(m == nm - 1): t.matmul(
                            PSd[b][:, 512 * jh:512 * jh + 512],
                            hq_fm[m][:, 128 * n:128 * n + 128],
                            wdT[m][:, 512 * jh:512 * jh + 512],
                            start=stt, stop=sp))
                        for jh in range(2) for m in range(nm)],
                        reads=[f"hqfm{m}" for m in range(nm)] +
                              [f"wdT{m}" for m in range(nm)],
                        writes=[kps])
                    P.op("v", lambda v, n=n, b=b: v.tensor_tensor(
                        acc[n][:, :], acc[n][:, :], PSd[b][:, :], ALU.add),
                        reads=[kps, f"acc{n}"], writes=[f"acc{n}"])

                def emit_hq_tail(ib, p):
                    i0, iw = IBLKS[ib]
                    pp = p % 2
                    emit_quant_tail(P, 2 * iw, f"hqb{pp}", hqb[pp][:, 0:2 * iw],
                                    scrs[pp])
                    for half in range(2):
                        n = 2 * p + half
                        P.dma("G", hq_bf[128 * n:128 * n + 128, i0:i0 + iw],
                              hqb[pp][:, iw * half:iw * half + iw],
                              reads=[f"hqb{pp}"], writes=[f"hq{ib}_{n}"])

                NB_ = len(IBLKS)

                def load_gup_w(ib):
                    i0, iw = IBLKS[ib]
                    wp = ib % 2
                    for c in range(CCH):
                        P.dma("Y", wgT[wp][c][:, 0:iw],
                              wgq_bf[i0:i0 + iw, 128 * c:128 * c + 128],
                              reads=WGK, writes=[f"wgT{c}_{wp}"], transpose=True)
                        P.dma("Y", wuT[wp][c][:, 0:iw],
                              wuq_bf[i0:i0 + iw, 128 * c:128 * c + 128],
                              reads=WUK, writes=[f"wuT{c}_{wp}"], transpose=True)

                load_gup_w(0)
                for ib in range(NB_ + 1):
                    if ib + 1 < NB_:
                        load_gup_w(ib + 1)
                    pend = None
                    for n in range(NT):
                        if ib >= 1 and ib == NB_:
                            emit_down(ib - 1, n)
                        if ib < NB_:
                            emit_gup(ib, n)
                            if ib >= 1 and n >= 2:
                                emit_down(ib - 1, n - 2)
                            if n % 2 == 1:
                                p = n // 2
                                i0, iw = IBLKS[ib]
                                emit_quant_head(P, 2 * iw, f"h32{p % 2}",
                                                h32[p % 2][:, 0:2 * iw],
                                                scrs[p % 2])
                                if pend is not None:
                                    emit_hq_tail(ib, pend)
                                pend = p
                        if ib == NB_:
                            b = n % 2
                            P.dma("Y", xf[b][:, :], x[128 * n:128 * n + 128, :],
                                  writes=[f"xf{b}", f"qE{b}s"])
                            P.op("v", lambda v, n=n, b=b: v.tensor_tensor(
                                ot[b][:, :], acc[n][:, :], xf[b][:, :], ALU.add),
                                reads=[f"acc{n}", f"xf{b}"],
                                writes=[f"ot{b}", f"qE{b}aa"])
                            P.dma("Y", out[128 * n:128 * n + 128, :],
                                  ot[b][:, :], reads=[f"ot{b}"],
                                  writes=[f"out{n}"])
                    if 1 <= ib < NB_:
                        emit_down(ib - 1, NT - 2)
                        emit_down(ib - 1, NT - 1)
                    if pend is not None:
                        emit_hq_tail(ib, pend)
                    if ib < NB_:
                        i0, iw = IBLKS[ib]
                        nm = iw // 128
                        for m in range(nm):
                            P.dma("S", wdT[m][:, :],
                                  wdq_bf[:, i0 + 128 * m:i0 + 128 * m + 128],
                                  reads=WDK, writes=[f"wdT{m}"], transpose=True)
                        for m in range(nm):
                            P.dma("Y", hq_fm[m][:, :],
                                  hq_bf[:, i0 + 128 * m:i0 + 128 * m + 128],
                                  reads=[f"hq{ib}_{n}" for n in range(NT)],
                                  writes=[f"hqfm{m}"], transpose=True)
                P.barrier()

    ctx.close()
    return nc, dbg_outs


def _rope_tables():
    n = D // 2
    j = np.arange(n, dtype=np.float32)
    freqs = np.float32(1.0) * (np.float32(40.0) ** (j / np.float32(n - 1)))
    pos = np.arange(T, dtype=np.float32) / np.float32(T)
    theta = np.float32(2.0 * np.pi) * pos[:, None] * freqs[None, :]
    return np.cos(theta).astype(np.float32), np.sin(theta).astype(np.float32)


def _rowind():
    r = np.zeros((80, H * 128), dtype=np.float32)
    for h in range(H):
        r[h, 128 * h:128 * h + 128] = 1.0
        r[64 + h, 128 * h:128 * h + 128] = 1.0
    return r


def make_in_maps(L0_attn_x, L0_attn_qkv_w, L0_attn_qkv_b, L0_attn_proj_w,
                 L0_attn_proj_b, L0_ffn_w_gate, L0_ffn_w_up, L0_ffn_w_down):
    xf = np.ascontiguousarray(np.asarray(L0_attn_x, dtype=np.float32))
    cost, sint = _rope_tables()
    common = {
        "qkv_w": np.ascontiguousarray(np.asarray(L0_attn_qkv_w, np.float32)),
        "proj_w": np.ascontiguousarray(np.asarray(L0_attn_proj_w, np.float32)),
        "wg": np.ascontiguousarray(np.asarray(L0_ffn_w_gate, np.float32)),
        "wu": np.ascontiguousarray(np.asarray(L0_ffn_w_up, np.float32)),
        "wd": np.ascontiguousarray(np.asarray(L0_ffn_w_down, np.float32)),
        "qkvb": np.ascontiguousarray(np.asarray(L0_attn_qkv_b, np.float32)
                                     .reshape(1, 3 * C)),
        "projb": np.ascontiguousarray(np.asarray(L0_attn_proj_b, np.float32)
                                      .reshape(1, C)),
        "costm": cost, "sintm": sint,
        "iden": np.eye(128, dtype=np.float32),
        "rowind": _rowind(),
    }
    return [dict(common, x=np.ascontiguousarray(xf[i])) for i in range(8)]


def kernel(L0_attn_x, L0_attn_qkv_w, L0_attn_qkv_b, L0_attn_proj_w,
           L0_attn_proj_b, L0_ffn_w_gate, L0_ffn_w_up, L0_ffn_w_down):
    in_maps = make_in_maps(L0_attn_x, L0_attn_qkv_w, L0_attn_qkv_b,
                           L0_attn_proj_w, L0_attn_proj_b, L0_ffn_w_gate,
                           L0_ffn_w_up, L0_ffn_w_down)
    nc, _ = build_program()
    res = run_bass_kernel_spmd(nc, in_maps, core_ids=list(range(8)))
    return np.stack([r["out"] for r in res.results], axis=0)


# version 23
# speedup vs baseline: 1.0339x; 1.0339x over previous
from contextlib import ExitStack

import numpy as np
import ml_dtypes
import concourse.bass as bass
import concourse.mybir as mybir
from concourse.bass_utils import run_bass_kernel_spmd

F32 = mybir.dt.float32
BF16 = mybir.dt.bfloat16
U8 = mybir.dt.uint8
ALU = mybir.AluOpType
AF = mybir.ActivationFunctionType
AX = mybir.AxisListType

T, C, H, D, I = 2048, 1024, 16, 64, 2816
NT = T // 128
CCH = C // 128
CV = float(2 ** 22 + 1)
CR = 12582912.0
IBLKS = [(0, 512), (512, 512), (1024, 512), (1536, 512), (2048, 512), (2560, 256)]


class PR:

    ENG = ("t", "v", "s", "g", "y")

    def __init__(self, nc):
        self.nc = nc
        self.eng = {"t": nc.tensor, "v": nc.vector, "s": nc.scalar,
                    "g": nc.gpsimd, "y": nc.sync}
        self.sem = {k: nc.alloc_semaphore(name=f"es_{k}") for k in ("t", "v", "s", "g")}
        self.cnt = {k: 0 for k in ("t", "v", "s", "g")}
        self.dsem = {q: [nc.alloc_semaphore(name=f"ds_{q}{i}") for i in range(8)]
                     for q in ("Y", "G", "S")}
        self.dcnt = {q: [0] * 8 for q in ("Y", "G", "S")}
        self.didx = {q: 0 for q in ("Y", "G", "S")}
        self.wmax = {k: {} for k in self.ENG}
        self.writer = {}
        self.readers = {}

    def _wait(self, ek, toks):
        eng = self.eng[ek]
        best = {}
        for (sem, val) in toks:
            if val <= 0:
                continue
            k = sem.num
            if val > best.get(k, (None, 0))[1]:
                best[k] = (sem, val)
        for k, (sem, val) in sorted(best.items()):
            if self.wmax[ek].get(k, 0) >= val:
                continue
            eng.wait_ge(sem, val)
            self.wmax[ek][k] = val

    def _deps(self, reads, writes):
        toks = []
        for k in reads:
            t = self.writer.get(k)
            if t:
                toks.append(t)
        for k in writes:
            t = self.writer.get(k)
            if t:
                toks.append(t)
            toks.extend(self.readers.get(k, {}).values())
        return toks

    def _record(self, tok, ek, reads, writes):
        for k in writes:
            self.writer[k] = tok
            self.readers[k] = {}
        for k in reads:
            self.readers.setdefault(k, {})[ek] = tok

    def op(self, ek, emit, reads=(), writes=()):
        self._wait(ek, self._deps(reads, writes))
        inst = emit(self.eng[ek])
        self.cnt[ek] += 1
        inst.then_inc(self.sem[ek], 1)
        tok = (self.sem[ek], self.cnt[ek])
        self._record(tok, ek, reads, writes)
        return tok

    def group(self, ek, emits, reads=(), writes=()):
        self._wait(ek, self._deps(reads, writes))
        eng = self.eng[ek]
        inst = None
        for e in emits:
            inst = e(eng)
        self.cnt[ek] += 1
        inst.then_inc(self.sem[ek], 1)
        tok = (self.sem[ek], self.cnt[ek])
        self._record(tok, ek, reads, writes)
        return tok

    def dma(self, qk, out, in_, reads=(), writes=(), transpose=False):
        ek = {"Y": "y", "G": "g", "S": "s"}[qk]
        self._wait(ek, self._deps(reads, writes))
        i = self.didx[qk]
        self.didx[qk] = (i + 1) % 8
        sem = self.dsem[qk][i]
        inst = self.eng[ek].dma_start(out=out, in_=in_, transpose=transpose) \
            if transpose else self.eng[ek].dma_start(out=out, in_=in_)
        self.dcnt[qk][i] += 16
        inst.then_inc(sem, 16)
        tok = (sem, self.dcnt[qk][i])
        self._record(tok, ek, reads, writes)
        return tok

    def barrier(self):
        toks = [(self.sem[k], self.cnt[k]) for k in ("t", "v", "s", "g")]
        for q in ("Y", "G", "S"):
            for i in range(8):
                toks.append((self.dsem[q][i], self.dcnt[q][i]))
        for ek in self.ENG:
            self._wait(ek, toks)


def emit_quant_head(P, W, src_key, src_ap, scr, act_recip=False):
    nb = W // 16
    tag = scr["tag"]
    red, sh, rsh, s = scr["red"], scr["sh"], scr["rsh"], scr["s"]
    kred, ksh, krsh, ks = tag + "red", tag + "sh", tag + "rsh", tag + "s"
    src3 = src_ap.rearrange("p (nb w) -> p nb w", w=16)
    P.op("v", lambda v: v.tensor_reduce(red[:, 0:nb], src3, AX.X, ALU.max,
                                        apply_absolute_value=True),
         reads=[src_key], writes=[kred])
    P.op("v", lambda v: v.tensor_scalar(sh[:, 0:nb], red[:, 0:nb], 1.0 / 12.0,
                                        5e-13, ALU.mult, ALU.max),
         reads=[kred], writes=[ksh])
    if act_recip:
        P.op("s", lambda sc: sc.activation(rsh[:, 0:nb], sh[:, 0:nb], AF.Ln),
             reads=[ksh], writes=[krsh])
        P.op("s", lambda sc: sc.activation(rsh[:, 0:nb], rsh[:, 0:nb], AF.Exp,
                                           scale=-1.0),
             reads=[krsh], writes=[krsh])
    else:
        P.op("v", lambda v: v.reciprocal(rsh[:, 0:nb], sh[:, 0:nb]),
             reads=[ksh], writes=[krsh])
    rb = rsh[:, 0:nb].unsqueeze(2).broadcast_to([128, nb, 16])
    s3 = s[:, 0:W].rearrange("p (nb w) -> p nb w", w=16)
    P.op("g", lambda g: g.tensor_tensor(s3, src3, rb, ALU.mult),
         reads=[src_key, krsh], writes=[ks])


def emit_quant_tail(P, W, dst_key, dst_ap, scr):
    nb = W // 16
    tag = scr["tag"]
    sh, s, aa, n1, n2 = scr["sh"], scr["s"], scr["aa"], scr["n1"], scr["n2"]
    sg = scr["sg"]
    ksh, ks, ka, kn1, kn2, ksg = (tag + "sh", tag + "s", tag + "aa",
                                  tag + "n1", tag + "n2", tag + "sg")
    shb = sh[:, 0:nb].unsqueeze(2).broadcast_to([128, nb, 16])
    P.op("s", lambda sc: sc.activation(aa[:, 0:W], s[:, 0:W], AF.Abs),
         reads=[ks], writes=[ka])
    P.op("s", lambda sc: sc.activation(sg[:, 0:W], s[:, 0:W], AF.Sign),
         reads=[ks], writes=[ksg])
    P.op("v", lambda v: v.tensor_scalar(n1[:, 0:W], aa[:, 0:W], 0.5, None,
                                        ALU.is_gt),
         reads=[ka], writes=[kn1])
    for thr in (1.5, 2.5, 3.5):
        P.op("v", lambda v, thr=thr: v.scalar_tensor_tensor(
            n1[:, 0:W], aa[:, 0:W], thr, n1[:, 0:W], ALU.is_gt, ALU.add),
            reads=[ka, kn1], writes=[kn1])
    P.op("v", lambda v: v.tensor_scalar(n2[:, 0:W], aa[:, 0:W], 5.0, None,
                                        ALU.is_gt),
         reads=[ka], writes=[kn2])
    for thr in (7.0, 10.0, 10.0):
        P.op("v", lambda v, thr=thr: v.scalar_tensor_tensor(
            n2[:, 0:W], aa[:, 0:W], thr, n2[:, 0:W], ALU.is_gt, ALU.add),
            reads=[ka, kn2], writes=[kn2])
    P.op("v", lambda v: v.scalar_tensor_tensor(
        n1[:, 0:W], n2[:, 0:W], 2.0, n1[:, 0:W], ALU.mult, ALU.add),
        reads=[kn1, kn2], writes=[kn1])
    P.op("v", lambda v: v.tensor_tensor(n1[:, 0:W], n1[:, 0:W], sg[:, 0:W],
                                        ALU.mult),
         reads=[kn1, ksg], writes=[kn1])
    qs3 = n1[:, 0:W].rearrange("p (nb w) -> p nb w", w=16)
    d3 = dst_ap.rearrange("p (nb w) -> p nb w", w=16)
    P.op("g", lambda g: g.tensor_tensor(d3, qs3, shb, ALU.mult),
         reads=[kn1, ksh], writes=[dst_key])


def emit_quant(P, W, src_key, src_ap, dst_key, dst_ap, scr, act_recip=False):
    emit_quant_head(P, W, src_key, src_ap, scr, act_recip=act_recip)
    emit_quant_tail(P, W, dst_key, dst_ap, scr)


def quant_scratch(alloc, pre, W):
    return dict(tag=pre,
                red=alloc(pre + "red", [128, W // 16], F32),
                sh=alloc(pre + "sh", [128, W // 16], F32),
                rsh=alloc(pre + "rsh", [128, W // 16], F32),
                s=alloc(pre + "s", [128, W], F32),
                aa=alloc(pre + "aa", [128, W], F32),
                n1=alloc(pre + "n1", [128, W], F32),
                n2=alloc(pre + "n2", [128, W], F32),
                sg=alloc(pre + "sg", [128, W], BF16))


def quant_scratch2(alloc, pre, W):
    return [quant_scratch(alloc, pre + str(i), W) for i in range(2)]


def build_program(dbg=frozenset(), phases=("A", "W", "B", "C", "D", "E", "F")):
    nc = bass.Bass("TRN2", target_bir_lowering=False)
    P = PR(nc)

    x = nc.dram_tensor("x", [T, C], F32, kind="ExternalInput").ap()
    qkv_w = nc.dram_tensor("qkv_w", [3 * C, C], F32, kind="ExternalInput").ap()
    proj_w = nc.dram_tensor("proj_w", [C, C], F32, kind="ExternalInput").ap()
    wg = nc.dram_tensor("wg", [I, C], F32, kind="ExternalInput").ap()
    wu = nc.dram_tensor("wu", [I, C], F32, kind="ExternalInput").ap()
    wd = nc.dram_tensor("wd", [C, I], F32, kind="ExternalInput").ap()
    qkvb = nc.dram_tensor("qkvb", [1, 3 * C], F32, kind="ExternalInput").ap()
    projb = nc.dram_tensor("projb", [1, C], F32, kind="ExternalInput").ap()
    costm = nc.dram_tensor("costm", [T, 32], F32, kind="ExternalInput").ap()
    sintm = nc.dram_tensor("sintm", [T, 32], F32, kind="ExternalInput").ap()
    iden = nc.dram_tensor("iden", [128, 128], F32, kind="ExternalInput").ap()
    rowind = nc.dram_tensor("rowind", [80, H * 128], F32, kind="ExternalInput").ap()
    out = nc.dram_tensor("out", [T, C], F32, kind="ExternalOutput").ap()

    xn_bf = nc.dram_tensor("xn_bf", [T, C], BF16, kind="Internal").ap()
    xq_bf = nc.dram_tensor("xq_bf", [T, C], BF16, kind="Internal").ap()
    qkvw_bf = nc.dram_tensor("qkvw_bf", [3 * C, C], BF16, kind="Internal").ap()
    projw_bf = nc.dram_tensor("projw_bf", [C, C], BF16, kind="Internal").ap()
    wgq_bf = nc.dram_tensor("wgq_bf", [I, C], BF16, kind="Internal").ap()
    wuq_bf = nc.dram_tensor("wuq_bf", [I, C], BF16, kind="Internal").ap()
    wdq_bf = nc.dram_tensor("wdq_bf", [C, I], BF16, kind="Internal").ap()
    q_bfd = nc.dram_tensor("q_bfd", [T, C], BF16, kind="Internal").ap()
    k_bfd = nc.dram_tensor("k_bfd", [T, C], BF16, kind="Internal").ap()
    hq_bf = nc.dram_tensor("hq_bf", [T, I], BF16, kind="Internal").ap()

    den_d = [nc.dram_tensor(f"den_d{i}", [1, 512], F32, kind="Internal").ap()
             for i in range(2)]
    dbg_outs = {}

    def dbg_dump(name, src_ap, after_keys):
        if name not in dbg:
            return
        o = nc.dram_tensor("dbg_" + name, list(src_ap.shape), src_ap.dtype,
                           kind="ExternalOutput").ap()
        P.dma("Y", o, src_ap, reads=list(after_keys), writes=["dbg_" + name])
        dbg_outs[name] = "dbg_" + name

    ctx = ExitStack()

    def SB(st, name, shape, dt, **kw):
        return st.enter_context(nc.sbuf_tensor(name, shape, dt, **kw))

    def PS(st, name, shape):
        return st.enter_context(nc.psum_tensor(name, shape, F32))

    with nc.Block():
        identsb = SB(ctx, "identsb", [128, 128], F32)
        projb_b = SB(ctx, "projb_b", [128, C], F32)
        ones164 = SB(ctx, "ones164", [1, 64], F32)
        P.dma("Y", identsb[:, :], iden, writes=["identsb"])
        P.dma("Y", projb_b[:, :], projb.broadcast_to([128, C]), writes=["projb_b"])
        P.op("v", lambda v: v.memset(ones164[:, :], 1.0), writes=["ones164"])
        P.dma("G", qkvw_bf, qkv_w, writes=["qkvw_bf"])
        P.dma("G", projw_bf, proj_w, writes=["projw_bf"])

        if "A" in phases:
            with ExitStack() as st:
                al = lambda n, sh, dt: SB(st, n, sh, dt)
                xt = [al(f"xa{i}", [128, C], F32) for i in range(2)]
                xn32 = [al(f"xn32{i}", [128, C], F32) for i in range(2)]
                xnb = al("xnb", [128, C], BF16)
                qb = [al(f"qb{i}", [128, C], BF16) for i in range(2)]
                sx = [al(f"sx_{i}", [128, 1], F32) for i in range(2)]
                sx2 = [al(f"sx2_{i}", [128, 1], F32) for i in range(2)]
                mvar = [al(f"mvar{i}", [128, 4], F32) for i in range(2)]
                mb = [al(f"mb{i}", [128, 1], F32) for i in range(2)]
                scrs = quant_scratch2(al, "qA", C)
                for n in range(NT):
                    b = n % 2
                    kx = f"xa{b}"
                    P.dma("Y", xt[b][:, :], x[128 * n:128 * n + 128, :], writes=[kx])
                    P.op("v", lambda v, b=b: v.tensor_reduce(
                        sx[b][:, :], xt[b][:, :], AX.X, ALU.add),
                        reads=[kx], writes=[f"sx{b}"])
                    P.op("s", lambda sc, b=b: sc.activation(
                        scrs[b]["aa"][:, :], xt[b][:, :], AF.Square,
                        accum_out=sx2[b][:, :]),
                        reads=[kx], writes=[f"qA{b}aa", f"sx2{b}"])
                    P.op("v", lambda v: v.tensor_scalar(
                        mvar[b][:, 0:1], sx[b][:, :], 1.0 / C, None, ALU.mult),
                        reads=[f"sx{b}"], writes=[f"m{b}"])
                    P.op("v", lambda v: v.tensor_scalar(
                        mvar[b][:, 1:2], sx2[b][:, :], 1.0 / C, None, ALU.mult),
                        reads=[f"sx2{b}"], writes=[f"ex2{b}"])
                    P.op("s", lambda sc: sc.activation(
                        mvar[b][:, 3:4], mvar[b][:, 0:1], AF.Square),
                        reads=[f"m{b}"], writes=[f"m2{b}"])
                    P.op("v", lambda v: v.scalar_tensor_tensor(
                        mvar[b][:, 2:3], mvar[b][:, 1:2], 1e-5, mvar[b][:, 3:4],
                        ALU.add, ALU.subtract),
                        reads=[f"ex2{b}", f"m2{b}"], writes=[f"var{b}"])
                    P.op("s", lambda sc: sc.activation(
                        mvar[b][:, 2:3], mvar[b][:, 2:3], AF.Sqrt),
                        reads=[f"var{b}"], writes=[f"var{b}"])
                    P.op("v", lambda v: v.reciprocal(mvar[b][:, 3:4], mvar[b][:, 2:3]),
                        reads=[f"var{b}", f"m2{b}"], writes=[f"rstd{b}"])
                    P.op("v", lambda v: v.scalar_tensor_tensor(
                        mb[b][:, :], mvar[b][:, 0:1], -1.0, mvar[b][:, 3:4],
                        ALU.mult, ALU.mult),
                        reads=[f"m{b}", f"rstd{b}"], writes=[f"mb{b}"])
                    P.op("s", lambda sc, b=b: sc.activation(
                        xn32[b][:, :], xt[b][:, :], AF.Identity,
                        bias=mb[b][:, :], scale=mvar[b][:, 3:4]),
                        reads=[kx, f"mb{b}", f"rstd{b}"], writes=[f"xn32{b}"])
                    P.op("s", lambda sc, b=b: sc.activation(
                        xnb[:, :], xt[b][:, :], AF.Identity,
                        bias=mb[b][:, :], scale=mvar[b][:, 3:4]),
                        reads=[kx, f"mb{b}", f"rstd{b}"], writes=["xnb"])
                    P.dma("G", xn_bf[128 * n:128 * n + 128, :], xnb[:, :],
                          reads=["xnb"], writes=[f"xn_bf{n}"])
                    emit_quant_head(P, C, f"xn32{b}", xn32[b][:, :], scrs[b])
                    if n > 0:
                        bp = (n - 1) % 2
                        emit_quant_tail(P, C, f"qb{bp}", qb[bp][:, :], scrs[bp])
                        P.dma("G", xq_bf[128 * (n - 1):128 * n, :],
                              qb[bp][:, :], reads=[f"qb{bp}"],
                              writes=[f"xq_bf{n - 1}"])
                bp = (NT - 1) % 2
                emit_quant_tail(P, C, f"qb{bp}", qb[bp][:, :], scrs[bp])
                P.dma("G", xq_bf[128 * (NT - 1):128 * NT, :], qb[bp][:, :],
                      reads=[f"qb{bp}"], writes=[f"xq_bf{NT - 1}"])
                P.barrier()
            dbg_dump("xn_bf", xn_bf, [f"xn_bf{n}" for n in range(NT)])
            dbg_dump("xq_bf", xq_bf, [f"xq_bf{n}" for n in range(NT)])

        stQ = ExitStack()
        stKV = ExitStack()
        q_fm = [SB(stQ, f"qfm{c}", [128, T], BF16) for c in range(CCH)]
        k_fm = [SB(stKV, f"kfm{c}", [128, T], BF16) for c in range(CCH)]
        vone = [SB(stKV, f"vone{n}", [128, H * 65], BF16) for n in range(NT)]
        NM = SB(stKV, "NM", [80, T], BF16)
        rowbf = SB(stKV, "rowbf", [80, H * 128], BF16)

        wjobs = []
        for r in range(I // 128):
            wjobs.append((wg, wgq_bf, r, 0, C, f"wgq{r}"))
            wjobs.append((wu, wuq_bf, r, 0, C, f"wuq{r}"))
        for r in range(C // 128):
            for j, (c0, cw) in enumerate([(0, 1024), (1024, 1024), (2048, 768)]):
                wjobs.append((wd, wdq_bf, r, c0, cw, f"wdq{r}_{j}"))
        WGK = [f"wgq{r}" for r in range(I // 128)]
        WUK = [f"wuq{r}" for r in range(I // 128)]
        WDK = [f"wdq{r}_{j}" for r in range(C // 128) for j in range(3)]

        if "B" in phases:
            with ExitStack() as st:
                al = lambda n, sh, dt: SB(st, n, sh, dt)
                xn_fm = [al(f"xnfm{c}", [128, T], BF16) for c in range(CCH)]
                wstrip = [al(f"wstr{c}", [128, 512], BF16) for c in range(CCH)]
                bstrip = al("bstrip", [128, 512], F32)
                cost = [al(f"cost{i}", [128, 32], F32) for i in range(2)]
                sint = [al(f"sint{i}", [128, 32], F32) for i in range(2)]
                qk32 = [al(f"qk32_{i}", [128, 512], F32) for i in range(2)]
                t1 = [al(f"t1_{i}", [128, 256], F32) for i in range(2)]
                t2 = [al(f"t2_{i}", [128, 256], F32) for i in range(2)]
                t3 = [al(f"t3_{i}", [128, 256], F32) for i in range(2)]
                t4 = [al(f"t4_{i}", [128, 256], F32) for i in range(2)]
                qtmb = [al(f"qtmb{i}", [128, 512], BF16) for i in range(2)]
                qsq = [al(f"qsq{i}", [128, 512], F32) for i in range(2)]
                nrm = [al(f"nrm{i}", [128, 8], F32) for i in range(2)]
                qnorm_all = al("qnorm_all", [128, H * NT], F32)
                kmax_acc = al("kmax_acc", [128, H], F32)
                kmx = al("kmx", [16, 1], F32)
                kng = al("kng", [80, 1], F32)
                c560 = al("c560", [80, 1], F32)
                PSqk = [PS(st, f"PSqk{i}", [128, 512]) for i in range(2)]
                PSt = PS(st, "PSt", [128, 128])

                for n in range(NT):
                    P.op("g", lambda g, n=n: g.memset(vone[n][:, :], 1.0),
                         writes=[f"vone{n}"])
                P.op("v", lambda v: v.memset(kmax_acc[:, :], 0.0), writes=["kmacc"])
                P.op("v", lambda v: v.memset(c560[:, :], 560.0), writes=["c560"])
                P.dma("G", rowbf[:, :], rowind, writes=["rowbf"])
                for c in range(CCH):
                    P.dma("Y", xn_fm[c][:, :], xn_bf[:, 128 * c:128 * c + 128],
                          reads=[f"xn_bf{n}" for n in range(NT)],
                          writes=[f"xnfm{c}"], transpose=True)

                for fs in range(6):
                    for c in range(CCH):
                        P.dma("Y", wstrip[c][:, :],
                              qkvw_bf[512 * fs:512 * fs + 512,
                                      128 * c:128 * c + 128],
                              reads=["qkvw_bf"], writes=[f"wstr{c}"],
                              transpose=True)
                    P.dma("G", bstrip[:, :],
                          qkvb[:, 512 * fs:512 * fs + 512].broadcast_to([128, 512]),
                          writes=["bstrip"])
                    for n in range(NT):
                        b = n % 2
                        kps = f"PSqk{b}"
                        P.group("t", [
                            (lambda t, c=c, b=b, n=n,
                             first=(c == 0), last=(c == CCH - 1): t.matmul(
                                PSqk[b][:, :], xn_fm[c][:, 128 * n:128 * n + 128],
                                wstrip[c][:, :], start=first, stop=last))
                            for c in range(CCH)],
                            reads=[f"xnfm{c}" for c in range(CCH)] +
                                  [f"wstr{c}" for c in range(CCH)],
                            writes=[kps])
                        if fs < 4:
                            P.dma("Y", cost[b][:, :], costm[128 * n:128 * n + 128, :],
                                  writes=[f"cost{b}"])
                            P.dma("Y", sint[b][:, :], sintm[128 * n:128 * n + 128, :],
                                  writes=[f"sint{b}"])
                            P.op("v", lambda v, b=b: v.tensor_tensor(
                                qk32[b][:, :], PSqk[b][:, :], bstrip[:, :], ALU.add),
                                reads=[kps, "bstrip"], writes=[f"qk32{b}"])
                            q3 = qk32[b][:, :].rearrange("p (h d) -> p h d", h=8)
                            cb = cost[b][:, :].unsqueeze(1).broadcast_to([128, 8, 32])
                            sbb = sint[b][:, :].unsqueeze(1).broadcast_to([128, 8, 32])
                            t13 = t1[b][:, :].rearrange("p (h d) -> p h d", h=8)
                            t23 = t2[b][:, :].rearrange("p (h d) -> p h d", h=8)
                            t33 = t3[b][:, :].rearrange("p (h d) -> p h d", h=8)
                            t43 = t4[b][:, :].rearrange("p (h d) -> p h d", h=8)
                            o3 = qtmb[b][:, :].rearrange("p (h d) -> p h d", h=8)
                            P.op("v", lambda v, q3=q3, cb=cb, t13=t13:
                                 v.tensor_tensor(t13, q3[:, :, 0:32], cb, ALU.mult),
                                 reads=[f"qk32{b}", f"cost{b}"], writes=[f"t1{b}"])
                            P.op("v", lambda v, q3=q3, sbb=sbb, t23=t23:
                                 v.tensor_tensor(t23, q3[:, :, 32:64], sbb, ALU.mult),
                                 reads=[f"qk32{b}", f"sint{b}"], writes=[f"t2{b}"])
                            P.op("v", lambda v, q3=q3, sbb=sbb, t33=t33:
                                 v.tensor_tensor(t33, q3[:, :, 0:32], sbb, ALU.mult),
                                 reads=[f"qk32{b}", f"sint{b}"], writes=[f"t3{b}"])
                            P.op("v", lambda v, q3=q3, cb=cb, t43=t43:
                                 v.tensor_tensor(t43, q3[:, :, 32:64], cb, ALU.mult),
                                 reads=[f"qk32{b}", f"cost{b}"], writes=[f"t4{b}"])
                            P.op("v", lambda v, o3=o3, t13=t13, t23=t23:
                                 v.tensor_tensor(o3[:, :, 0:32], t13, t23,
                                                 ALU.subtract),
                                 reads=[f"t1{b}", f"t2{b}"], writes=[f"qtmb{b}"])
                            P.op("v", lambda v, o3=o3, t33=t33, t43=t43:
                                 v.tensor_tensor(o3[:, :, 32:64], t33, t43, ALU.add),
                                 reads=[f"t3{b}", f"t4{b}"], writes=[f"qtmb{b}"])
                            dstd = q_bfd if fs < 2 else k_bfd
                            fo = 512 * fs if fs < 2 else 512 * (fs - 2)
                            kname = ("q_bfd" if fs < 2 else "k_bfd") + f"{fs}_{n}"
                            P.dma("G", dstd[128 * n:128 * n + 128, fo:fo + 512],
                                  qtmb[b][:, :], reads=[f"qtmb{b}"], writes=[kname])
                            P.op("s", lambda sc, b=b: sc.activation(
                                qsq[b][:, :], qtmb[b][:, :], AF.Square),
                                reads=[f"qtmb{b}"], writes=[f"qsq{b}"])
                            qs3 = qsq[b][:, :].rearrange("p (h d) -> p h d", h=8)
                            P.op("v", lambda v, qs3=qs3, b=b: v.tensor_reduce(
                                nrm[b][:, :], qs3, AX.X, ALU.add),
                                reads=[f"qsq{b}"], writes=[f"nrm{b}"])
                            if fs < 2:
                                co = H * n + 8 * fs
                                P.op("s", lambda sc, co=co, b=b: sc.activation(
                                    qnorm_all[:, co:co + 8], nrm[b][:, :],
                                    AF.Sqrt),
                                    reads=[f"nrm{b}"], writes=["qna"])
                            else:
                                P.op("s", lambda sc, b=b: sc.activation(
                                    nrm[b][:, :], nrm[b][:, :], AF.Sqrt),
                                    reads=[f"nrm{b}"], writes=[f"nrm{b}"])
                                co = 8 * (fs - 2)
                                P.op("v", lambda v, co=co, b=b: v.tensor_tensor(
                                    kmax_acc[:, co:co + 8], kmax_acc[:, co:co + 8],
                                    nrm[b][:, :], ALU.max),
                                    reads=[f"nrm{b}", "kmacc"], writes=["kmacc"])
                        else:
                            h0 = 8 * (fs - 4)
                            pv3 = PSqk[b][:, :].rearrange("p (h d) -> p h d", h=8)
                            bb3 = bstrip[:, :].rearrange("p (h d) -> p h d", h=8)
                            dst3 = vone[n][:, 65 * h0:65 * h0 + 520].rearrange(
                                "p (h e) -> p h e", e=65)[:, :, 0:64]
                            P.op("v", lambda v, pv3=pv3, bb3=bb3, dst3=dst3:
                                 v.tensor_tensor(dst3, pv3, bb3, ALU.add),
                                 reads=[kps, "bstrip", f"vone{n}"],
                                 writes=[f"vone{n}"])
                if "qna" in dbg:
                    qo = nc.dram_tensor("dbg_qna", [128, H * NT], F32,
                                        kind="ExternalOutput").ap()
                    P.dma("Y", qo, qnorm_all[:, :], reads=["qna"],
                          writes=["dbg_qna"])
                    dbg_outs["qna"] = "dbg_qna"
                P.group("t", [lambda t: t.transpose(
                    PSt[0:16, :], kmax_acc[:, :], identsb[:, :])],
                    reads=["kmacc", "identsb"], writes=["PSt"])
                P.op("v", lambda v: v.tensor_reduce(kmx[:, :], PSt[0:16, :],
                                                    AX.X, ALU.max),
                     reads=["PSt"], writes=["kmx"])
                P.op("v", lambda v: v.tensor_scalar(
                    kng[0:16, :], kmx[:, :], -1.0, None, ALU.mult),
                    reads=["kmx"], writes=["kng"])
                P.op("v", lambda v: v.tensor_copy(kng[64:80, :], kng[0:16, :]),
                     reads=["kng"], writes=["kng"])
                for n in range(NT):
                    P.group("t", [lambda t, n=n: t.transpose(
                        PSt[0:16, :], qnorm_all[:, H * n:H * n + H],
                        identsb[:, :])],
                        reads=["qna", "identsb"], writes=["PSt"])
                    P.op("s", lambda sc, n=n: sc.activation(
                        NM[0:16, 128 * n:128 * n + 128], PSt[0:16, :],
                        AF.Identity, bias=c560[0:16, :], scale=kng[0:16, :]),
                        reads=["PSt", "kng", "c560"], writes=["NM"])
                    P.op("v", lambda v, n=n: v.tensor_copy(
                        NM[64:80, 128 * n:128 * n + 128],
                        NM[0:16, 128 * n:128 * n + 128]),
                        reads=["NM"], writes=["NM"])
                P.barrier()
            for c in range(CCH):
                P.dma("Y", q_fm[c][:, :], q_bfd[:, 128 * c:128 * c + 128],
                      reads=[f"q_bfd{fs}_{n}" for fs in range(2)
                             for n in range(NT)],
                      writes=[f"qfm{c}"], transpose=True)
                P.dma("Y", k_fm[c][:, :], k_bfd[:, 128 * c:128 * c + 128],
                      reads=[f"k_bfd{fs}_{n}" for fs in range(2, 4)
                             for n in range(NT)],
                      writes=[f"kfm{c}"], transpose=True)
            dbg_dump("q_bfd", q_bfd,
                     [f"q_bfd{fs}_{n}" for fs in range(2) for n in range(NT)])
            dbg_dump("k_bfd", k_bfd,
                     [f"k_bfd{fs}_{n}" for fs in range(2, 4) for n in range(NT)])

        if "C" in phases:
            with ExitStack() as st:
                al = lambda n, sh, dt: SB(st, n, sh, dt)
                pT = [al(f"pT{i}", [128, 1024], BF16) for i in range(2)]
                rrow = [al(f"rrow{i}", [1, 512], F32) for i in range(2)]
                rb = [al(f"rb{i}", [64, 512], F32) for i in range(2)]
                wt_w = [al(f"wt{i}", [128, 1024], F32) for i in range(4)]
                wqb_w = [al(f"wqb{i}", [128, 1024], BF16) for i in range(2)]
                wscrs = quant_scratch2(al, "qW", 1024)
                PSs = [PS(st, f"PSs{i}", [128, 1024]) for i in range(2)]
                PSp = [[PS(st, f"PSpa{i}", [128, 512]),
                        PS(st, f"PSpb{i}", [128, 512])] for i in range(2)]
                _wj = {"l": 0, "h": 0, "t": 0}

                def _wload(i):
                    src, dst, r, c0, cw, key = wjobs[i]
                    b = i % 4
                    P.dma("Y", wt_w[b][:, 0:cw],
                          src[128 * r:128 * r + 128, c0:c0 + cw],
                          writes=[f"wt{b}"])

                def _whead(i):
                    src, dst, r, c0, cw, key = wjobs[i]
                    b = i % 4
                    emit_quant_head(P, cw, f"wt{b}", wt_w[b][:, 0:cw],
                                    wscrs[i % 2])

                def _wtail(i):
                    src, dst, r, c0, cw, key = wjobs[i]
                    b = i % 2
                    emit_quant_tail(P, cw, f"wqb{b}", wqb_w[b][:, 0:cw],
                                    wscrs[b])
                    P.dma("G", dst[128 * r:128 * r + 128, c0:c0 + cw],
                          wqb_w[b][:, 0:cw], reads=[f"wqb{b}"], writes=[key])

                def emit_wjobs(k):
                    while k > 0 and _wj["t"] < len(wjobs):
                        while _wj["l"] < min(len(wjobs), _wj["t"] + 4):
                            _wload(_wj["l"])
                            _wj["l"] += 1
                        while _wj["h"] < min(len(wjobs), _wj["t"] + 2):
                            _whead(_wj["h"])
                            _wj["h"] += 1
                        _wtail(_wj["t"])
                        _wj["t"] += 1
                        k -= 1

                while _wj["l"] < 4:
                    _wload(_wj["l"])
                    _wj["l"] += 1

                def emit_S(hp, qb, kt):
                    b = kt % 2
                    P.group("t", [
                        (lambda t: t.matmul(
                            PSs[b][:, 0:512],
                            k_fm[hp][0:64, 128 * kt:128 * kt + 128],
                            q_fm[hp][0:64, 512 * qb:512 * qb + 512],
                            start=True, stop=False, tile_position=(0, 0))),
                        (lambda t: t.matmul(
                            PSs[b][:, 512:1024],
                            k_fm[hp][64:128, 128 * kt:128 * kt + 128],
                            q_fm[hp][64:128, 512 * qb:512 * qb + 512],
                            start=True, stop=False, tile_position=(64, 0))),
                        (lambda t: t.matmul(
                            PSs[b][:, 0:512],
                            rowbf[0:16, 128 * (2 * hp):128 * (2 * hp) + 128],
                            NM[0:16, 512 * qb:512 * qb + 512],
                            start=False, stop=True, tile_position=(0, 0))),
                        (lambda t: t.matmul(
                            PSs[b][:, 512:1024],
                            rowbf[64:80,
                                  128 * (2 * hp + 1):128 * (2 * hp + 1) + 128],
                            NM[64:80, 512 * qb:512 * qb + 512],
                            start=False, stop=True, tile_position=(64, 0)))],
                        reads=[f"qfm{hp}", f"kfm{hp}", "NM", "rowbf",
                               f"qfm{hp}_a_{qb}", f"qfm{hp}_b_{qb}"],
                        writes=[f"PSs{b}"])
                    P.op("s", lambda sc: sc.activation(
                        pT[b][:, :], PSs[b][:, :], AF.Exp, scale=0.125),
                        reads=[f"PSs{b}"], writes=[f"pT{b}"])

                def emit_PV(hp, kt, j):
                    b = kt % 2
                    pa, pb = PSp[j % 2]
                    P.group("t", [
                        (lambda t: t.matmul(
                            pa[0:65, :],
                            vone[kt][:, 130 * hp:130 * hp + 65],
                            pT[b][:, 0:512],
                            start=(kt == 0), stop=(kt == NT - 1))),
                        (lambda t: t.matmul(
                            pb[0:65, :],
                            vone[kt][:, 130 * hp + 65:130 * hp + 130],
                            pT[b][:, 512:1024],
                            start=(kt == 0), stop=(kt == NT - 1)))],
                        reads=[f"pT{b}", f"vone{kt}"],
                        writes=[f"PSpa{j % 2}", f"PSpb{j % 2}"])

                def emit_evac_recips(hp, qb, j):
                    for hh in range(2):
                        Pp, kP = PSp[j % 2][hh], f"PSp{'ab'[hh]}{j % 2}"
                        P.op("v", lambda v, Pp=Pp, hh=hh: v.reciprocal(
                            rrow[hh][:, :], Pp[64:65, :]),
                            reads=[kP], writes=[f"rrow{hh}"])
                        P.dma("G", den_d[hh], rrow[hh][:, :],
                              reads=[f"rrow{hh}"], writes=[f"den_d{hh}"])
                        P.dma("Y", rb[hh][:, :],
                              den_d[hh].broadcast_to([64, 512]),
                              reads=[f"den_d{hh}"], writes=[f"rb{hh}"])

                def emit_evac_mults(hp, qb, j):
                    for hh in range(2):
                        Pp, kP = PSp[j % 2][hh], f"PSp{'ab'[hh]}{j % 2}"
                        ro = 64 * hh
                        P.op("v", lambda v, Pp=Pp, hp=hp, qb=qb, ro=ro, hh=hh:
                             v.tensor_tensor(
                                 q_fm[hp][ro:ro + 64, 512 * qb:512 * qb + 512],
                                 Pp[0:64, :], rb[hh][:, :], ALU.mult),
                             reads=[kP, f"rb{hh}"],
                             writes=[f"qfm{hp}_{'ab'[hh]}_{qb}"])

                blocks = [(hp, qb) for hp in range(CCH) for qb in range(4)]
                nbk = len(blocks)
                nwj = len(wjobs)
                for j, (hp, qb) in enumerate(blocks):
                    quota = (nwj * (j + 1)) // nbk - (nwj * j) // nbk
                    for kt in range(NT):
                        emit_S(hp, qb, kt)
                        if kt == 0 and j > 0:
                            emit_evac_recips(*blocks[j - 1], j - 1)
                        if kt > 0:
                            emit_PV(hp, kt - 1, j)
                        if kt == 1 and quota >= 1:
                            emit_wjobs(1)
                        if kt == 4 and j > 0:
                            emit_evac_mults(*blocks[j - 1], j - 1)
                        if kt == 7 and quota >= 2:
                            emit_wjobs(1)
                        if kt == 12 and quota >= 3:
                            emit_wjobs(quota - 2)
                    emit_PV(hp, NT - 1, j)
                emit_evac_recips(*blocks[-1], len(blocks) - 1)
                emit_evac_mults(*blocks[-1], len(blocks) - 1)
                emit_wjobs(len(wjobs))
                P.barrier()
            if "ao_d" in dbg:
                ao_d = nc.dram_tensor("dbg_ao_d", [C, T], BF16,
                                      kind="ExternalOutput").ap()
                for c in range(CCH):
                    P.dma("Y", ao_d[128 * c:128 * c + 128, :], q_fm[c][:, :],
                          reads=[f"qfm{c}_{ab}_{qb}" for ab in "ab"
                                 for qb in range(4)],
                          writes=["dbg_ao_d"])
                dbg_outs["ao_d"] = "dbg_ao_d"
            if "negm" in dbg:
                nm_d = nc.dram_tensor("dbg_negm", [80, T], BF16,
                                      kind="ExternalOutput").ap()
                P.dma("Y", nm_d, NM[:, :], reads=["NM"], writes=["dbg_negm"])
                dbg_outs["negm"] = "dbg_negm"
        stKV.close()

        acc = [SB(ctx, f"acc{n}", [128, C], F32, side="right") for n in range(NT)]
        if "D" in phases:
            with ExitStack() as st:
                pwT = [SB(st, f"pwT{c}", [128, C], BF16) for c in range(CCH)]
                PSpj = [PS(st, f"PSpj{i}", [128, 512]) for i in range(2)]
                for c in range(CCH):
                    P.dma("Y", pwT[c][:, :], projw_bf[:, 128 * c:128 * c + 128],
                          reads=["projw_bf"], writes=[f"pwT{c}"], transpose=True)
                aokeys = [f"qfm{c}_{ab}_{qb}" for c in range(CCH)
                          for ab in "ab" for qb in range(4)]
                for n in range(NT):
                    for jh in range(2):
                        b = jh
                        kps = f"PSpj{b}"
                        P.group("t", [
                            (lambda t, c=c, n=n, jh=jh, b=b,
                             first=(c == 0), last=(c == CCH - 1): t.matmul(
                                PSpj[b][:, :], q_fm[c][:, 128 * n:128 * n + 128],
                                pwT[c][:, 512 * jh:512 * jh + 512],
                                start=first, stop=last))
                            for c in range(CCH)],
                            reads=aokeys + [f"pwT{c}" for c in range(CCH)],
                            writes=[kps])
                        P.op("v", lambda v, n=n, jh=jh, b=b: v.tensor_tensor(
                            acc[n][:, 512 * jh:512 * jh + 512], PSpj[b][:, :],
                            projb_b[:, 512 * jh:512 * jh + 512], ALU.add),
                            reads=[kps, "projb_b"], writes=[f"acc{n}"])
                P.barrier()
            if "acc_d" in dbg:
                acc_d = nc.dram_tensor("dbg_acc_d", [T, C], F32,
                                       kind="ExternalOutput").ap()
                for n in range(NT):
                    P.dma("Y", acc_d[128 * n:128 * n + 128, :], acc[n][:, :],
                          reads=[f"acc{n}"], writes=["dbg_acc_d"])
                dbg_outs["acc_d"] = "dbg_acc_d"
        stQ.close()

        if "E" in phases:
            with ExitStack() as st:
                al = lambda n, sh, dt: SB(st, n, sh, dt)
                xq_fm = [al(f"xqfm{c}", [128, T], BF16) for c in range(CCH)]
                wgT = [[al(f"wgT{c}_{i}", [128, 512], BF16)
                        for c in range(CCH)] for i in range(2)]
                wuT = [[al(f"wuT{c}_{i}", [128, 512], BF16)
                        for c in range(CCH)] for i in range(2)]
                wdT = [al(f"wdT{m}", [128, C], BF16) for m in range(4)]
                hq_fm = [al(f"hqfm{m}", [128, T], BF16) for m in range(4)]
                sil = [al(f"sil{i}", [128, 512], F32) for i in range(2)]
                h32 = [al(f"h32{i}", [128, 1024], F32) for i in range(2)]
                scrs = quant_scratch2(al, "qE", 1024)
                hqb = [scrs[i]["sg"] for i in range(2)]
                xf = [scrs[0]["s"], scrs[1]["s"]]
                ot = [scrs[0]["aa"], scrs[1]["aa"]]
                PSg = [PS(st, f"PSg{i}", [128, 512]) for i in range(2)]
                PSu = [PS(st, f"PSu{i}", [128, 512]) for i in range(2)]
                PSd = [PS(st, f"PSd{i}", [128, C]) for i in range(2)]
                for c in range(CCH):
                    P.dma("Y", xq_fm[c][:, :], xq_bf[:, 128 * c:128 * c + 128],
                          reads=[f"xq_bf{n}" for n in range(NT)],
                          writes=[f"xqfm{c}"], transpose=True)

                def emit_gup(ib, n):
                    i0, iw = IBLKS[ib]
                    b = n % 2
                    wp = ib % 2
                    P.group("t", [
                        (lambda t, c=c, n=n, iw=iw, i0=i0, b=b, wp=wp,
                         first=(c == 0), last=(c == CCH - 1): t.matmul(
                            PSg[b][:, 0:iw], xq_fm[c][:, 128 * n:128 * n + 128],
                            wgT[wp][c][:, 0:iw], start=first, stop=last))
                        for c in range(CCH)],
                        reads=[f"xqfm{c}" for c in range(CCH)] +
                              [f"wgT{c}_{wp}" for c in range(CCH)],
                        writes=[f"PSg{b}"])
                    P.group("t", [
                        (lambda t, c=c, n=n, iw=iw, i0=i0, b=b, wp=wp,
                         first=(c == 0), last=(c == CCH - 1): t.matmul(
                            PSu[b][:, 0:iw], xq_fm[c][:, 128 * n:128 * n + 128],
                            wuT[wp][c][:, 0:iw], start=first, stop=last))
                        for c in range(CCH)],
                        reads=[f"xqfm{c}" for c in range(CCH)] +
                              [f"wuT{c}_{wp}" for c in range(CCH)],
                        writes=[f"PSu{b}"])
                    pp = (n // 2) % 2
                    ho = iw * (n % 2)
                    P.op("s", lambda sc, iw=iw, b=b: sc.activation(
                        sil[b][:, 0:iw], PSg[b][:, 0:iw], AF.Silu),
                        reads=[f"PSg{b}"], writes=[f"sil{b}"])
                    P.op("v", lambda v, iw=iw, b=b, pp=pp, ho=ho: v.tensor_tensor(
                        h32[pp][:, ho:ho + iw], sil[b][:, 0:iw],
                        PSu[b][:, 0:iw], ALU.mult),
                        reads=[f"sil{b}", f"PSu{b}"], writes=[f"h32{pp}"])

                def emit_down(ib, n):
                    i0, iw = IBLKS[ib]
                    nm = iw // 128
                    b = n % 2
                    kps = f"PSd{b}"
                    P.group("t", [
                        (lambda t, m=m, jh=jh, n=n, b=b,
                         stt=(m == 0), sp=(m == nm - 1): t.matmul(
                            PSd[b][:, 512 * jh:512 * jh + 512],
                            hq_fm[m][:, 128 * n:128 * n + 128],
                            wdT[m][:, 512 * jh:512 * jh + 512],
                            start=stt, stop=sp))
                        for jh in range(2) for m in range(nm)],
                        reads=[f"hqfm{m}" for m in range(nm)] +
                              [f"wdT{m}" for m in range(nm)],
                        writes=[kps])
                    P.op("v", lambda v, n=n, b=b: v.tensor_tensor(
                        acc[n][:, :], acc[n][:, :], PSd[b][:, :], ALU.add),
                        reads=[kps, f"acc{n}"], writes=[f"acc{n}"])

                def emit_hq_tail(ib, p):
                    i0, iw = IBLKS[ib]
                    pp = p % 2
                    emit_quant_tail(P, 2 * iw, f"hqb{pp}", hqb[pp][:, 0:2 * iw],
                                    scrs[pp])
                    for half in range(2):
                        n = 2 * p + half
                        P.dma("G", hq_bf[128 * n:128 * n + 128, i0:i0 + iw],
                              hqb[pp][:, iw * half:iw * half + iw],
                              reads=[f"hqb{pp}"], writes=[f"hq{ib}_{n}"])

                NB_ = len(IBLKS)

                def load_gup_w(ib):
                    i0, iw = IBLKS[ib]
                    wp = ib % 2
                    for c in range(CCH):
                        P.dma("Y", wgT[wp][c][:, 0:iw],
                              wgq_bf[i0:i0 + iw, 128 * c:128 * c + 128],
                              reads=WGK, writes=[f"wgT{c}_{wp}"], transpose=True)
                        P.dma("Y", wuT[wp][c][:, 0:iw],
                              wuq_bf[i0:i0 + iw, 128 * c:128 * c + 128],
                              reads=WUK, writes=[f"wuT{c}_{wp}"], transpose=True)

                load_gup_w(0)
                for ib in range(NB_ + 1):
                    if ib + 1 < NB_:
                        load_gup_w(ib + 1)
                    pend = None
                    for n in range(NT):
                        if ib >= 1 and ib == NB_:
                            emit_down(ib - 1, n)
                        if ib < NB_:
                            emit_gup(ib, n)
                            if ib >= 1 and n >= 2:
                                emit_down(ib - 1, n - 2)
                            if n % 2 == 1:
                                p = n // 2
                                i0, iw = IBLKS[ib]
                                emit_quant_head(P, 2 * iw, f"h32{p % 2}",
                                                h32[p % 2][:, 0:2 * iw],
                                                scrs[p % 2])
                                if pend is not None:
                                    emit_hq_tail(ib, pend)
                                pend = p
                        if ib == NB_:
                            b = n % 2
                            P.dma("S", xf[b][:, :], x[128 * n:128 * n + 128, :],
                                  writes=[f"xf{b}", f"qE{b}s"])
                            P.op("v", lambda v, n=n, b=b: v.tensor_tensor(
                                ot[b][:, :], acc[n][:, :], xf[b][:, :], ALU.add),
                                reads=[f"acc{n}", f"xf{b}"],
                                writes=[f"ot{b}", f"qE{b}aa"])
                            P.dma("Y", out[128 * n:128 * n + 128, :],
                                  ot[b][:, :], reads=[f"ot{b}"],
                                  writes=[f"out{n}"])
                    if 1 <= ib < NB_:
                        emit_down(ib - 1, NT - 2)
                        emit_down(ib - 1, NT - 1)
                    if pend is not None:
                        emit_hq_tail(ib, pend)
                    if ib < NB_:
                        i0, iw = IBLKS[ib]
                        nm = iw // 128
                        for m in range(nm):
                            P.dma("Y", wdT[m][:, :],
                                  wdq_bf[:, i0 + 128 * m:i0 + 128 * m + 128],
                                  reads=WDK, writes=[f"wdT{m}"], transpose=True)
                        for m in range(nm):
                            P.dma("Y", hq_fm[m][:, :],
                                  hq_bf[:, i0 + 128 * m:i0 + 128 * m + 128],
                                  reads=[f"hq{ib}_{n}" for n in range(NT)],
                                  writes=[f"hqfm{m}"], transpose=True)
                P.barrier()

    ctx.close()
    return nc, dbg_outs


def _rope_tables():
    n = D // 2
    j = np.arange(n, dtype=np.float32)
    freqs = np.float32(1.0) * (np.float32(40.0) ** (j / np.float32(n - 1)))
    pos = np.arange(T, dtype=np.float32) / np.float32(T)
    theta = np.float32(2.0 * np.pi) * pos[:, None] * freqs[None, :]
    return np.cos(theta).astype(np.float32), np.sin(theta).astype(np.float32)


def _rowind():
    r = np.zeros((80, H * 128), dtype=np.float32)
    for h in range(H):
        r[h, 128 * h:128 * h + 128] = 1.0
        r[64 + h, 128 * h:128 * h + 128] = 1.0
    return r


def make_in_maps(L0_attn_x, L0_attn_qkv_w, L0_attn_qkv_b, L0_attn_proj_w,
                 L0_attn_proj_b, L0_ffn_w_gate, L0_ffn_w_up, L0_ffn_w_down):
    xf = np.ascontiguousarray(np.asarray(L0_attn_x, dtype=np.float32))
    cost, sint = _rope_tables()
    common = {
        "qkv_w": np.ascontiguousarray(np.asarray(L0_attn_qkv_w, np.float32)),
        "proj_w": np.ascontiguousarray(np.asarray(L0_attn_proj_w, np.float32)),
        "wg": np.ascontiguousarray(np.asarray(L0_ffn_w_gate, np.float32)),
        "wu": np.ascontiguousarray(np.asarray(L0_ffn_w_up, np.float32)),
        "wd": np.ascontiguousarray(np.asarray(L0_ffn_w_down, np.float32)),
        "qkvb": np.ascontiguousarray(np.asarray(L0_attn_qkv_b, np.float32)
                                     .reshape(1, 3 * C)),
        "projb": np.ascontiguousarray(np.asarray(L0_attn_proj_b, np.float32)
                                      .reshape(1, C)),
        "costm": cost, "sintm": sint,
        "iden": np.eye(128, dtype=np.float32),
        "rowind": _rowind(),
    }
    return [dict(common, x=np.ascontiguousarray(xf[i])) for i in range(8)]


def kernel(L0_attn_x, L0_attn_qkv_w, L0_attn_qkv_b, L0_attn_proj_w,
           L0_attn_proj_b, L0_ffn_w_gate, L0_ffn_w_up, L0_ffn_w_down):
    in_maps = make_in_maps(L0_attn_x, L0_attn_qkv_w, L0_attn_qkv_b,
                           L0_attn_proj_w, L0_attn_proj_b, L0_ffn_w_gate,
                           L0_ffn_w_up, L0_ffn_w_down)
    nc, _ = build_program()
    res = run_bass_kernel_spmd(nc, in_maps, core_ids=list(range(8)))
    return np.stack([r["out"] for r in res.results], axis=0)


# version 24 (final)
# speedup vs baseline: 1.0358x; 1.0019x over previous
from contextlib import ExitStack

import numpy as np
import ml_dtypes
import concourse.bass as bass
import concourse.mybir as mybir
from concourse.bass_utils import run_bass_kernel_spmd

F32 = mybir.dt.float32
BF16 = mybir.dt.bfloat16
U8 = mybir.dt.uint8
ALU = mybir.AluOpType
AF = mybir.ActivationFunctionType
AX = mybir.AxisListType

T, C, H, D, I = 2048, 1024, 16, 64, 2816
NT = T // 128
CCH = C // 128
CV = float(2 ** 22 + 1)
CR = 12582912.0
IBLKS = [(0, 512), (512, 512), (1024, 512), (1536, 512), (2048, 512), (2560, 256)]


class PR:

    ENG = ("t", "v", "s", "g", "y")

    def __init__(self, nc):
        self.nc = nc
        self.eng = {"t": nc.tensor, "v": nc.vector, "s": nc.scalar,
                    "g": nc.gpsimd, "y": nc.sync}
        self.sem = {k: nc.alloc_semaphore(name=f"es_{k}") for k in ("t", "v", "s", "g")}
        self.cnt = {k: 0 for k in ("t", "v", "s", "g")}
        self.dsem = {q: [nc.alloc_semaphore(name=f"ds_{q}{i}") for i in range(8)]
                     for q in ("Y", "G", "S")}
        self.dcnt = {q: [0] * 8 for q in ("Y", "G", "S")}
        self.didx = {q: 0 for q in ("Y", "G", "S")}
        self.wmax = {k: {} for k in self.ENG}
        self.writer = {}
        self.readers = {}

    def _wait(self, ek, toks):
        eng = self.eng[ek]
        best = {}
        for (sem, val) in toks:
            if val <= 0:
                continue
            k = sem.num
            if val > best.get(k, (None, 0))[1]:
                best[k] = (sem, val)
        for k, (sem, val) in sorted(best.items()):
            if self.wmax[ek].get(k, 0) >= val:
                continue
            eng.wait_ge(sem, val)
            self.wmax[ek][k] = val

    def _deps(self, reads, writes):
        toks = []
        for k in reads:
            t = self.writer.get(k)
            if t:
                toks.append(t)
        for k in writes:
            t = self.writer.get(k)
            if t:
                toks.append(t)
            toks.extend(self.readers.get(k, {}).values())
        return toks

    def _record(self, tok, ek, reads, writes):
        for k in writes:
            self.writer[k] = tok
            self.readers[k] = {}
        for k in reads:
            self.readers.setdefault(k, {})[ek] = tok

    def op(self, ek, emit, reads=(), writes=()):
        self._wait(ek, self._deps(reads, writes))
        inst = emit(self.eng[ek])
        self.cnt[ek] += 1
        inst.then_inc(self.sem[ek], 1)
        tok = (self.sem[ek], self.cnt[ek])
        self._record(tok, ek, reads, writes)
        return tok

    def group(self, ek, emits, reads=(), writes=()):
        self._wait(ek, self._deps(reads, writes))
        eng = self.eng[ek]
        inst = None
        for e in emits:
            inst = e(eng)
        self.cnt[ek] += 1
        inst.then_inc(self.sem[ek], 1)
        tok = (self.sem[ek], self.cnt[ek])
        self._record(tok, ek, reads, writes)
        return tok

    def dma(self, qk, out, in_, reads=(), writes=(), transpose=False):
        ek = {"Y": "y", "G": "g", "S": "s"}[qk]
        self._wait(ek, self._deps(reads, writes))
        i = self.didx[qk]
        self.didx[qk] = (i + 1) % 8
        sem = self.dsem[qk][i]
        inst = self.eng[ek].dma_start(out=out, in_=in_, transpose=transpose) \
            if transpose else self.eng[ek].dma_start(out=out, in_=in_)
        self.dcnt[qk][i] += 16
        inst.then_inc(sem, 16)
        tok = (sem, self.dcnt[qk][i])
        self._record(tok, ek, reads, writes)
        return tok

    def barrier(self):
        toks = [(self.sem[k], self.cnt[k]) for k in ("t", "v", "s", "g")]
        for q in ("Y", "G", "S"):
            for i in range(8):
                toks.append((self.dsem[q][i], self.dcnt[q][i]))
        for ek in self.ENG:
            self._wait(ek, toks)


def emit_quant_head(P, W, src_key, src_ap, scr, act_recip=False):
    nb = W // 16
    tag = scr["tag"]
    red, sh, rsh, s = scr["red"], scr["sh"], scr["rsh"], scr["s"]
    kred, ksh, krsh, ks = tag + "red", tag + "sh", tag + "rsh", tag + "s"
    src3 = src_ap.rearrange("p (nb w) -> p nb w", w=16)
    P.op("v", lambda v: v.tensor_reduce(red[:, 0:nb], src3, AX.X, ALU.max,
                                        apply_absolute_value=True),
         reads=[src_key], writes=[kred])
    P.op("v", lambda v: v.tensor_scalar(sh[:, 0:nb], red[:, 0:nb], 1.0 / 12.0,
                                        5e-13, ALU.mult, ALU.max),
         reads=[kred], writes=[ksh])
    if act_recip:
        P.op("s", lambda sc: sc.activation(rsh[:, 0:nb], sh[:, 0:nb], AF.Ln),
             reads=[ksh], writes=[krsh])
        P.op("s", lambda sc: sc.activation(rsh[:, 0:nb], rsh[:, 0:nb], AF.Exp,
                                           scale=-1.0),
             reads=[krsh], writes=[krsh])
    else:
        P.op("v", lambda v: v.reciprocal(rsh[:, 0:nb], sh[:, 0:nb]),
             reads=[ksh], writes=[krsh])
    rb = rsh[:, 0:nb].unsqueeze(2).broadcast_to([128, nb, 16])
    s3 = s[:, 0:W].rearrange("p (nb w) -> p nb w", w=16)
    P.op("g", lambda g: g.tensor_tensor(s3, src3, rb, ALU.mult),
         reads=[src_key, krsh], writes=[ks])


def emit_quant_tail(P, W, dst_key, dst_ap, scr):
    nb = W // 16
    tag = scr["tag"]
    sh, s, aa, n1, n2 = scr["sh"], scr["s"], scr["aa"], scr["n1"], scr["n2"]
    sg = scr["sg"]
    ksh, ks, ka, kn1, kn2, ksg = (tag + "sh", tag + "s", tag + "aa",
                                  tag + "n1", tag + "n2", tag + "sg")
    shb = sh[:, 0:nb].unsqueeze(2).broadcast_to([128, nb, 16])
    P.op("s", lambda sc: sc.activation(aa[:, 0:W], s[:, 0:W], AF.Abs),
         reads=[ks], writes=[ka])
    P.op("s", lambda sc: sc.activation(sg[:, 0:W], s[:, 0:W], AF.Sign),
         reads=[ks], writes=[ksg])
    P.op("v", lambda v: v.tensor_scalar(n1[:, 0:W], aa[:, 0:W], 0.5, None,
                                        ALU.is_gt),
         reads=[ka], writes=[kn1])
    for thr in (1.5, 2.5, 3.5):
        P.op("v", lambda v, thr=thr: v.scalar_tensor_tensor(
            n1[:, 0:W], aa[:, 0:W], thr, n1[:, 0:W], ALU.is_gt, ALU.add),
            reads=[ka, kn1], writes=[kn1])
    P.op("v", lambda v: v.tensor_scalar(n2[:, 0:W], aa[:, 0:W], 5.0, None,
                                        ALU.is_gt),
         reads=[ka], writes=[kn2])
    for thr in (7.0, 10.0, 10.0):
        P.op("v", lambda v, thr=thr: v.scalar_tensor_tensor(
            n2[:, 0:W], aa[:, 0:W], thr, n2[:, 0:W], ALU.is_gt, ALU.add),
            reads=[ka, kn2], writes=[kn2])
    P.op("v", lambda v: v.scalar_tensor_tensor(
        n1[:, 0:W], n2[:, 0:W], 2.0, n1[:, 0:W], ALU.mult, ALU.add),
        reads=[kn1, kn2], writes=[kn1])
    P.op("v", lambda v: v.tensor_tensor(n1[:, 0:W], n1[:, 0:W], sg[:, 0:W],
                                        ALU.mult),
         reads=[kn1, ksg], writes=[kn1])
    qs3 = n1[:, 0:W].rearrange("p (nb w) -> p nb w", w=16)
    d3 = dst_ap.rearrange("p (nb w) -> p nb w", w=16)
    P.op("g", lambda g: g.tensor_tensor(d3, qs3, shb, ALU.mult),
         reads=[kn1, ksh], writes=[dst_key])


def emit_quant(P, W, src_key, src_ap, dst_key, dst_ap, scr, act_recip=False):
    emit_quant_head(P, W, src_key, src_ap, scr, act_recip=act_recip)
    emit_quant_tail(P, W, dst_key, dst_ap, scr)


def quant_scratch(alloc, pre, W):
    return dict(tag=pre,
                red=alloc(pre + "red", [128, W // 16], F32),
                sh=alloc(pre + "sh", [128, W // 16], F32),
                rsh=alloc(pre + "rsh", [128, W // 16], F32),
                s=alloc(pre + "s", [128, W], F32),
                aa=alloc(pre + "aa", [128, W], F32),
                n1=alloc(pre + "n1", [128, W], F32),
                n2=alloc(pre + "n2", [128, W], F32),
                sg=alloc(pre + "sg", [128, W], BF16))


def quant_scratch2(alloc, pre, W):
    return [quant_scratch(alloc, pre + str(i), W) for i in range(2)]


def build_program(dbg=frozenset(), phases=("A", "W", "B", "C", "D", "E", "F")):
    nc = bass.Bass("TRN2", target_bir_lowering=False)
    P = PR(nc)

    x = nc.dram_tensor("x", [T, C], F32, kind="ExternalInput").ap()
    qkv_w = nc.dram_tensor("qkv_w", [3 * C, C], F32, kind="ExternalInput").ap()
    proj_w = nc.dram_tensor("proj_w", [C, C], F32, kind="ExternalInput").ap()
    wg = nc.dram_tensor("wg", [I, C], F32, kind="ExternalInput").ap()
    wu = nc.dram_tensor("wu", [I, C], F32, kind="ExternalInput").ap()
    wd = nc.dram_tensor("wd", [C, I], F32, kind="ExternalInput").ap()
    qkvb = nc.dram_tensor("qkvb", [1, 3 * C], F32, kind="ExternalInput").ap()
    projb = nc.dram_tensor("projb", [1, C], F32, kind="ExternalInput").ap()
    costm = nc.dram_tensor("costm", [T, 32], F32, kind="ExternalInput").ap()
    sintm = nc.dram_tensor("sintm", [T, 32], F32, kind="ExternalInput").ap()
    iden = nc.dram_tensor("iden", [128, 128], F32, kind="ExternalInput").ap()
    rowind = nc.dram_tensor("rowind", [80, H * 128], F32, kind="ExternalInput").ap()
    out = nc.dram_tensor("out", [T, C], F32, kind="ExternalOutput").ap()

    xn_bf = nc.dram_tensor("xn_bf", [T, C], BF16, kind="Internal").ap()
    xq_bf = nc.dram_tensor("xq_bf", [T, C], BF16, kind="Internal").ap()
    qkvw_bf = nc.dram_tensor("qkvw_bf", [3 * C, C], BF16, kind="Internal").ap()
    projw_bf = nc.dram_tensor("projw_bf", [C, C], BF16, kind="Internal").ap()
    wgq_bf = nc.dram_tensor("wgq_bf", [I, C], BF16, kind="Internal").ap()
    wuq_bf = nc.dram_tensor("wuq_bf", [I, C], BF16, kind="Internal").ap()
    wdq_bf = nc.dram_tensor("wdq_bf", [C, I], BF16, kind="Internal").ap()
    q_bfd = nc.dram_tensor("q_bfd", [T, C], BF16, kind="Internal").ap()
    k_bfd = nc.dram_tensor("k_bfd", [T, C], BF16, kind="Internal").ap()
    hq_bf = nc.dram_tensor("hq_bf", [T, I], BF16, kind="Internal").ap()

    den_d = [nc.dram_tensor(f"den_d{i}", [1, 512], F32, kind="Internal").ap()
             for i in range(2)]
    dbg_outs = {}

    def dbg_dump(name, src_ap, after_keys):
        if name not in dbg:
            return
        o = nc.dram_tensor("dbg_" + name, list(src_ap.shape), src_ap.dtype,
                           kind="ExternalOutput").ap()
        P.dma("Y", o, src_ap, reads=list(after_keys), writes=["dbg_" + name])
        dbg_outs[name] = "dbg_" + name

    ctx = ExitStack()

    def SB(st, name, shape, dt, **kw):
        return st.enter_context(nc.sbuf_tensor(name, shape, dt, **kw))

    def PS(st, name, shape):
        return st.enter_context(nc.psum_tensor(name, shape, F32))

    with nc.Block():
        identsb = SB(ctx, "identsb", [128, 128], F32)
        projb_b = SB(ctx, "projb_b", [128, C], F32)
        ones164 = SB(ctx, "ones164", [1, 64], F32)
        P.dma("Y", identsb[:, :], iden, writes=["identsb"])
        P.dma("Y", projb_b[:, :], projb.broadcast_to([128, C]), writes=["projb_b"])
        P.op("v", lambda v: v.memset(ones164[:, :], 1.0), writes=["ones164"])
        P.dma("G", qkvw_bf, qkv_w, writes=["qkvw_bf"])
        P.dma("G", projw_bf, proj_w, writes=["projw_bf"])

        if "A" in phases:
            with ExitStack() as st:
                al = lambda n, sh, dt: SB(st, n, sh, dt)
                xt = [al(f"xa{i}", [128, C], F32) for i in range(2)]
                xn32 = [al(f"xn32{i}", [128, C], F32) for i in range(2)]
                xnb = al("xnb", [128, C], BF16)
                qb = [al(f"qb{i}", [128, C], BF16) for i in range(2)]
                sx = [al(f"sx_{i}", [128, 1], F32) for i in range(2)]
                sx2 = [al(f"sx2_{i}", [128, 1], F32) for i in range(2)]
                mvar = [al(f"mvar{i}", [128, 4], F32) for i in range(2)]
                mb = [al(f"mb{i}", [128, 1], F32) for i in range(2)]
                scrs = quant_scratch2(al, "qA", C)
                for n in range(NT):
                    b = n % 2
                    kx = f"xa{b}"
                    P.dma("Y", xt[b][:, :], x[128 * n:128 * n + 128, :], writes=[kx])
                    P.op("v", lambda v, b=b: v.tensor_reduce(
                        sx[b][:, :], xt[b][:, :], AX.X, ALU.add),
                        reads=[kx], writes=[f"sx{b}"])
                    P.op("s", lambda sc, b=b: sc.activation(
                        scrs[b]["aa"][:, :], xt[b][:, :], AF.Square,
                        accum_out=sx2[b][:, :]),
                        reads=[kx], writes=[f"qA{b}aa", f"sx2{b}"])
                    P.op("v", lambda v: v.tensor_scalar(
                        mvar[b][:, 0:1], sx[b][:, :], 1.0 / C, None, ALU.mult),
                        reads=[f"sx{b}"], writes=[f"m{b}"])
                    P.op("v", lambda v: v.tensor_scalar(
                        mvar[b][:, 1:2], sx2[b][:, :], 1.0 / C, None, ALU.mult),
                        reads=[f"sx2{b}"], writes=[f"ex2{b}"])
                    P.op("s", lambda sc: sc.activation(
                        mvar[b][:, 3:4], mvar[b][:, 0:1], AF.Square),
                        reads=[f"m{b}"], writes=[f"m2{b}"])
                    P.op("v", lambda v: v.scalar_tensor_tensor(
                        mvar[b][:, 2:3], mvar[b][:, 1:2], 1e-5, mvar[b][:, 3:4],
                        ALU.add, ALU.subtract),
                        reads=[f"ex2{b}", f"m2{b}"], writes=[f"var{b}"])
                    P.op("s", lambda sc: sc.activation(
                        mvar[b][:, 2:3], mvar[b][:, 2:3], AF.Sqrt),
                        reads=[f"var{b}"], writes=[f"var{b}"])
                    P.op("v", lambda v: v.reciprocal(mvar[b][:, 3:4], mvar[b][:, 2:3]),
                        reads=[f"var{b}", f"m2{b}"], writes=[f"rstd{b}"])
                    P.op("v", lambda v: v.scalar_tensor_tensor(
                        mb[b][:, :], mvar[b][:, 0:1], -1.0, mvar[b][:, 3:4],
                        ALU.mult, ALU.mult),
                        reads=[f"m{b}", f"rstd{b}"], writes=[f"mb{b}"])
                    P.op("s", lambda sc, b=b: sc.activation(
                        xn32[b][:, :], xt[b][:, :], AF.Identity,
                        bias=mb[b][:, :], scale=mvar[b][:, 3:4]),
                        reads=[kx, f"mb{b}", f"rstd{b}"], writes=[f"xn32{b}"])
                    P.op("s", lambda sc, b=b: sc.activation(
                        xnb[:, :], xt[b][:, :], AF.Identity,
                        bias=mb[b][:, :], scale=mvar[b][:, 3:4]),
                        reads=[kx, f"mb{b}", f"rstd{b}"], writes=["xnb"])
                    P.dma("G", xn_bf[128 * n:128 * n + 128, :], xnb[:, :],
                          reads=["xnb"], writes=[f"xn_bf{n}"])
                    emit_quant_head(P, C, f"xn32{b}", xn32[b][:, :], scrs[b])
                    if n > 0:
                        bp = (n - 1) % 2
                        emit_quant_tail(P, C, f"qb{bp}", qb[bp][:, :], scrs[bp])
                        P.dma("G", xq_bf[128 * (n - 1):128 * n, :],
                              qb[bp][:, :], reads=[f"qb{bp}"],
                              writes=[f"xq_bf{n - 1}"])
                bp = (NT - 1) % 2
                emit_quant_tail(P, C, f"qb{bp}", qb[bp][:, :], scrs[bp])
                P.dma("G", xq_bf[128 * (NT - 1):128 * NT, :], qb[bp][:, :],
                      reads=[f"qb{bp}"], writes=[f"xq_bf{NT - 1}"])
                P.barrier()
            dbg_dump("xn_bf", xn_bf, [f"xn_bf{n}" for n in range(NT)])
            dbg_dump("xq_bf", xq_bf, [f"xq_bf{n}" for n in range(NT)])

        stQ = ExitStack()
        stKV = ExitStack()
        q_fm = [SB(stQ, f"qfm{c}", [128, T], BF16) for c in range(CCH)]
        k_fm = [SB(stKV, f"kfm{c}", [128, T], BF16) for c in range(CCH)]
        vone = [SB(stKV, f"vone{n}", [128, H * 65], BF16) for n in range(NT)]
        NM = SB(stKV, "NM", [80, T], BF16)
        rowbf = SB(stKV, "rowbf", [80, H * 128], BF16)

        wjobs = []
        for r in range(I // 128):
            wjobs.append((wg, wgq_bf, r, 0, C, f"wgq{r}"))
            wjobs.append((wu, wuq_bf, r, 0, C, f"wuq{r}"))
        for r in range(C // 128):
            for j, (c0, cw) in enumerate([(0, 1024), (1024, 1024), (2048, 768)]):
                wjobs.append((wd, wdq_bf, r, c0, cw, f"wdq{r}_{j}"))
        WGK = [f"wgq{r}" for r in range(I // 128)]
        WUK = [f"wuq{r}" for r in range(I // 128)]
        WDK = [f"wdq{r}_{j}" for r in range(C // 128) for j in range(3)]

        if "B" in phases:
            with ExitStack() as st:
                al = lambda n, sh, dt: SB(st, n, sh, dt)
                xn_fm = [al(f"xnfm{c}", [128, T], BF16) for c in range(CCH)]
                wstrip = [al(f"wstr{c}", [128, 512], BF16) for c in range(CCH)]
                bstrip = al("bstrip", [128, 512], F32)
                cost = [al(f"cost{i}", [128, 32], F32) for i in range(2)]
                sint = [al(f"sint{i}", [128, 32], F32) for i in range(2)]
                qk32 = [al(f"qk32_{i}", [128, 512], F32) for i in range(2)]
                t1 = [al(f"t1_{i}", [128, 256], F32) for i in range(2)]
                t2 = [al(f"t2_{i}", [128, 256], F32) for i in range(2)]
                t3 = [al(f"t3_{i}", [128, 256], F32) for i in range(2)]
                t4 = [al(f"t4_{i}", [128, 256], F32) for i in range(2)]
                qtmb = [al(f"qtmb{i}", [128, 512], BF16) for i in range(2)]
                qsq = [al(f"qsq{i}", [128, 512], F32) for i in range(2)]
                nrm = [al(f"nrm{i}", [128, 8], F32) for i in range(2)]
                qnorm_all = al("qnorm_all", [128, H * NT], F32)
                kmax_acc = al("kmax_acc", [128, H], F32)
                kmx = al("kmx", [16, 1], F32)
                kng = al("kng", [80, 1], F32)
                c560 = al("c560", [80, 1], F32)
                PSqk = [PS(st, f"PSqk{i}", [128, 512]) for i in range(2)]
                PSt = PS(st, "PSt", [128, 128])

                for n in range(NT):
                    P.op("g", lambda g, n=n: g.memset(vone[n][:, :], 1.0),
                         writes=[f"vone{n}"])
                P.op("v", lambda v: v.memset(kmax_acc[:, :], 0.0), writes=["kmacc"])
                P.op("v", lambda v: v.memset(c560[:, :], 560.0), writes=["c560"])
                P.dma("G", rowbf[:, :], rowind, writes=["rowbf"])
                for c in range(CCH):
                    P.dma("Y", xn_fm[c][:, :], xn_bf[:, 128 * c:128 * c + 128],
                          reads=[f"xn_bf{n}" for n in range(NT)],
                          writes=[f"xnfm{c}"], transpose=True)

                for fs in range(6):
                    for c in range(CCH):
                        P.dma("Y", wstrip[c][:, :],
                              qkvw_bf[512 * fs:512 * fs + 512,
                                      128 * c:128 * c + 128],
                              reads=["qkvw_bf"], writes=[f"wstr{c}"],
                              transpose=True)
                    P.dma("G", bstrip[:, :],
                          qkvb[:, 512 * fs:512 * fs + 512].broadcast_to([128, 512]),
                          writes=["bstrip"])
                    for n in range(NT):
                        b = n % 2
                        kps = f"PSqk{b}"
                        P.group("t", [
                            (lambda t, c=c, b=b, n=n,
                             first=(c == 0), last=(c == CCH - 1): t.matmul(
                                PSqk[b][:, :], xn_fm[c][:, 128 * n:128 * n + 128],
                                wstrip[c][:, :], start=first, stop=last))
                            for c in range(CCH)],
                            reads=[f"xnfm{c}" for c in range(CCH)] +
                                  [f"wstr{c}" for c in range(CCH)],
                            writes=[kps])
                        if fs < 4:
                            P.dma("Y", cost[b][:, :], costm[128 * n:128 * n + 128, :],
                                  writes=[f"cost{b}"])
                            P.dma("Y", sint[b][:, :], sintm[128 * n:128 * n + 128, :],
                                  writes=[f"sint{b}"])
                            P.op("v", lambda v, b=b: v.tensor_tensor(
                                qk32[b][:, :], PSqk[b][:, :], bstrip[:, :], ALU.add),
                                reads=[kps, "bstrip"], writes=[f"qk32{b}"])
                            q3 = qk32[b][:, :].rearrange("p (h d) -> p h d", h=8)
                            cb = cost[b][:, :].unsqueeze(1).broadcast_to([128, 8, 32])
                            sbb = sint[b][:, :].unsqueeze(1).broadcast_to([128, 8, 32])
                            t13 = t1[b][:, :].rearrange("p (h d) -> p h d", h=8)
                            t23 = t2[b][:, :].rearrange("p (h d) -> p h d", h=8)
                            t33 = t3[b][:, :].rearrange("p (h d) -> p h d", h=8)
                            t43 = t4[b][:, :].rearrange("p (h d) -> p h d", h=8)
                            o3 = qtmb[b][:, :].rearrange("p (h d) -> p h d", h=8)
                            P.op("v", lambda v, q3=q3, cb=cb, t13=t13:
                                 v.tensor_tensor(t13, q3[:, :, 0:32], cb, ALU.mult),
                                 reads=[f"qk32{b}", f"cost{b}"], writes=[f"t1{b}"])
                            P.op("v", lambda v, q3=q3, sbb=sbb, t23=t23:
                                 v.tensor_tensor(t23, q3[:, :, 32:64], sbb, ALU.mult),
                                 reads=[f"qk32{b}", f"sint{b}"], writes=[f"t2{b}"])
                            P.op("v", lambda v, q3=q3, sbb=sbb, t33=t33:
                                 v.tensor_tensor(t33, q3[:, :, 0:32], sbb, ALU.mult),
                                 reads=[f"qk32{b}", f"sint{b}"], writes=[f"t3{b}"])
                            P.op("v", lambda v, q3=q3, cb=cb, t43=t43:
                                 v.tensor_tensor(t43, q3[:, :, 32:64], cb, ALU.mult),
                                 reads=[f"qk32{b}", f"cost{b}"], writes=[f"t4{b}"])
                            P.op("v", lambda v, o3=o3, t13=t13, t23=t23:
                                 v.tensor_tensor(o3[:, :, 0:32], t13, t23,
                                                 ALU.subtract),
                                 reads=[f"t1{b}", f"t2{b}"], writes=[f"qtmb{b}"])
                            P.op("v", lambda v, o3=o3, t33=t33, t43=t43:
                                 v.tensor_tensor(o3[:, :, 32:64], t33, t43, ALU.add),
                                 reads=[f"t3{b}", f"t4{b}"], writes=[f"qtmb{b}"])
                            dstd = q_bfd if fs < 2 else k_bfd
                            fo = 512 * fs if fs < 2 else 512 * (fs - 2)
                            kname = ("q_bfd" if fs < 2 else "k_bfd") + f"{fs}_{n}"
                            P.dma("G", dstd[128 * n:128 * n + 128, fo:fo + 512],
                                  qtmb[b][:, :], reads=[f"qtmb{b}"], writes=[kname])
                            P.op("s", lambda sc, b=b: sc.activation(
                                qsq[b][:, :], qtmb[b][:, :], AF.Square),
                                reads=[f"qtmb{b}"], writes=[f"qsq{b}"])
                            qs3 = qsq[b][:, :].rearrange("p (h d) -> p h d", h=8)
                            P.op("v", lambda v, qs3=qs3, b=b: v.tensor_reduce(
                                nrm[b][:, :], qs3, AX.X, ALU.add),
                                reads=[f"qsq{b}"], writes=[f"nrm{b}"])
                            if fs < 2:
                                co = H * n + 8 * fs
                                P.op("s", lambda sc, co=co, b=b: sc.activation(
                                    qnorm_all[:, co:co + 8], nrm[b][:, :],
                                    AF.Sqrt),
                                    reads=[f"nrm{b}"], writes=["qna"])
                            else:
                                P.op("s", lambda sc, b=b: sc.activation(
                                    nrm[b][:, :], nrm[b][:, :], AF.Sqrt),
                                    reads=[f"nrm{b}"], writes=[f"nrm{b}"])
                                co = 8 * (fs - 2)
                                P.op("v", lambda v, co=co, b=b: v.tensor_tensor(
                                    kmax_acc[:, co:co + 8], kmax_acc[:, co:co + 8],
                                    nrm[b][:, :], ALU.max),
                                    reads=[f"nrm{b}", "kmacc"], writes=["kmacc"])
                        else:
                            h0 = 8 * (fs - 4)
                            pv3 = PSqk[b][:, :].rearrange("p (h d) -> p h d", h=8)
                            bb3 = bstrip[:, :].rearrange("p (h d) -> p h d", h=8)
                            dst3 = vone[n][:, 65 * h0:65 * h0 + 520].rearrange(
                                "p (h e) -> p h e", e=65)[:, :, 0:64]
                            P.op("v", lambda v, pv3=pv3, bb3=bb3, dst3=dst3:
                                 v.tensor_tensor(dst3, pv3, bb3, ALU.add),
                                 reads=[kps, "bstrip", f"vone{n}"],
                                 writes=[f"vone{n}"])
                if "qna" in dbg:
                    qo = nc.dram_tensor("dbg_qna", [128, H * NT], F32,
                                        kind="ExternalOutput").ap()
                    P.dma("Y", qo, qnorm_all[:, :], reads=["qna"],
                          writes=["dbg_qna"])
                    dbg_outs["qna"] = "dbg_qna"
                P.group("t", [lambda t: t.transpose(
                    PSt[0:16, :], kmax_acc[:, :], identsb[:, :])],
                    reads=["kmacc", "identsb"], writes=["PSt"])
                P.op("v", lambda v: v.tensor_reduce(kmx[:, :], PSt[0:16, :],
                                                    AX.X, ALU.max),
                     reads=["PSt"], writes=["kmx"])
                P.op("v", lambda v: v.tensor_scalar(
                    kng[0:16, :], kmx[:, :], -1.0, None, ALU.mult),
                    reads=["kmx"], writes=["kng"])
                P.op("v", lambda v: v.tensor_copy(kng[64:80, :], kng[0:16, :]),
                     reads=["kng"], writes=["kng"])
                for n in range(NT):
                    P.group("t", [lambda t, n=n: t.transpose(
                        PSt[0:16, :], qnorm_all[:, H * n:H * n + H],
                        identsb[:, :])],
                        reads=["qna", "identsb"], writes=["PSt"])
                    P.op("s", lambda sc, n=n: sc.activation(
                        NM[0:16, 128 * n:128 * n + 128], PSt[0:16, :],
                        AF.Identity, bias=c560[0:16, :], scale=kng[0:16, :]),
                        reads=["PSt", "kng", "c560"], writes=["NM"])
                    P.op("v", lambda v, n=n: v.tensor_copy(
                        NM[64:80, 128 * n:128 * n + 128],
                        NM[0:16, 128 * n:128 * n + 128]),
                        reads=["NM"], writes=["NM"])
                P.barrier()
            for c in range(CCH):
                P.dma("Y", q_fm[c][:, :], q_bfd[:, 128 * c:128 * c + 128],
                      reads=[f"q_bfd{fs}_{n}" for fs in range(2)
                             for n in range(NT)],
                      writes=[f"qfm{c}"], transpose=True)
                P.dma("Y", k_fm[c][:, :], k_bfd[:, 128 * c:128 * c + 128],
                      reads=[f"k_bfd{fs}_{n}" for fs in range(2, 4)
                             for n in range(NT)],
                      writes=[f"kfm{c}"], transpose=True)
            dbg_dump("q_bfd", q_bfd,
                     [f"q_bfd{fs}_{n}" for fs in range(2) for n in range(NT)])
            dbg_dump("k_bfd", k_bfd,
                     [f"k_bfd{fs}_{n}" for fs in range(2, 4) for n in range(NT)])

        if "C" in phases:
            with ExitStack() as st:
                al = lambda n, sh, dt: SB(st, n, sh, dt)
                pT = [al(f"pT{i}", [128, 1024], BF16) for i in range(2)]
                rrow = [al(f"rrow{i}", [1, 512], F32) for i in range(2)]
                rb = [al(f"rb{i}", [64, 512], F32) for i in range(2)]
                wt_w = [al(f"wt{i}", [128, 1024], F32) for i in range(4)]
                wqb_w = [al(f"wqb{i}", [128, 1024], BF16) for i in range(2)]
                wscrs = quant_scratch2(al, "qW", 1024)
                PSs = [PS(st, f"PSs{i}", [128, 1024]) for i in range(2)]
                PSp = [[PS(st, f"PSpa{i}", [128, 512]),
                        PS(st, f"PSpb{i}", [128, 512])] for i in range(2)]
                _wj = {"l": 0, "h": 0, "t": 0}

                def _wload(i):
                    src, dst, r, c0, cw, key = wjobs[i]
                    b = i % 4
                    P.dma("Y", wt_w[b][:, 0:cw],
                          src[128 * r:128 * r + 128, c0:c0 + cw],
                          writes=[f"wt{b}"])

                def _whead(i):
                    src, dst, r, c0, cw, key = wjobs[i]
                    b = i % 4
                    emit_quant_head(P, cw, f"wt{b}", wt_w[b][:, 0:cw],
                                    wscrs[i % 2])

                def _wtail(i):
                    src, dst, r, c0, cw, key = wjobs[i]
                    b = i % 2
                    emit_quant_tail(P, cw, f"wqb{b}", wqb_w[b][:, 0:cw],
                                    wscrs[b])
                    P.dma("G", dst[128 * r:128 * r + 128, c0:c0 + cw],
                          wqb_w[b][:, 0:cw], reads=[f"wqb{b}"], writes=[key])

                def emit_wjobs(k):
                    while k > 0 and _wj["t"] < len(wjobs):
                        while _wj["l"] < min(len(wjobs), _wj["t"] + 4):
                            _wload(_wj["l"])
                            _wj["l"] += 1
                        while _wj["h"] < min(len(wjobs), _wj["t"] + 2):
                            _whead(_wj["h"])
                            _wj["h"] += 1
                        _wtail(_wj["t"])
                        _wj["t"] += 1
                        k -= 1

                while _wj["l"] < 4:
                    _wload(_wj["l"])
                    _wj["l"] += 1

                def emit_S(hp, qb, kt):
                    b = kt % 2
                    P.group("t", [
                        (lambda t: t.matmul(
                            PSs[b][:, 0:512],
                            k_fm[hp][0:64, 128 * kt:128 * kt + 128],
                            q_fm[hp][0:64, 512 * qb:512 * qb + 512],
                            start=True, stop=False, tile_position=(0, 0))),
                        (lambda t: t.matmul(
                            PSs[b][:, 512:1024],
                            k_fm[hp][64:128, 128 * kt:128 * kt + 128],
                            q_fm[hp][64:128, 512 * qb:512 * qb + 512],
                            start=True, stop=False, tile_position=(64, 0))),
                        (lambda t: t.matmul(
                            PSs[b][:, 0:512],
                            rowbf[0:16, 128 * (2 * hp):128 * (2 * hp) + 128],
                            NM[0:16, 512 * qb:512 * qb + 512],
                            start=False, stop=True, tile_position=(0, 0))),
                        (lambda t: t.matmul(
                            PSs[b][:, 512:1024],
                            rowbf[64:80,
                                  128 * (2 * hp + 1):128 * (2 * hp + 1) + 128],
                            NM[64:80, 512 * qb:512 * qb + 512],
                            start=False, stop=True, tile_position=(64, 0)))],
                        reads=[f"qfm{hp}", f"kfm{hp}", "NM", "rowbf",
                               f"qfm{hp}_a_{qb}", f"qfm{hp}_b_{qb}"],
                        writes=[f"PSs{b}"])
                    P.op("s", lambda sc: sc.activation(
                        pT[b][:, :], PSs[b][:, :], AF.Exp, scale=0.125),
                        reads=[f"PSs{b}"], writes=[f"pT{b}"])

                def emit_PV(hp, kt, j):
                    b = kt % 2
                    pa, pb = PSp[j % 2]
                    P.group("t", [
                        (lambda t: t.matmul(
                            pa[0:65, :],
                            vone[kt][:, 130 * hp:130 * hp + 65],
                            pT[b][:, 0:512],
                            start=(kt == 0), stop=(kt == NT - 1))),
                        (lambda t: t.matmul(
                            pb[0:65, :],
                            vone[kt][:, 130 * hp + 65:130 * hp + 130],
                            pT[b][:, 512:1024],
                            start=(kt == 0), stop=(kt == NT - 1)))],
                        reads=[f"pT{b}", f"vone{kt}"],
                        writes=[f"PSpa{j % 2}", f"PSpb{j % 2}"])

                def emit_evac_recips(hp, qb, j):
                    for hh in range(2):
                        Pp, kP = PSp[j % 2][hh], f"PSp{'ab'[hh]}{j % 2}"
                        P.op("v", lambda v, Pp=Pp, hh=hh: v.reciprocal(
                            rrow[hh][:, :], Pp[64:65, :]),
                            reads=[kP], writes=[f"rrow{hh}"])
                        P.dma("G", den_d[hh], rrow[hh][:, :],
                              reads=[f"rrow{hh}"], writes=[f"den_d{hh}"])
                        P.dma("Y", rb[hh][:, :],
                              den_d[hh].broadcast_to([64, 512]),
                              reads=[f"den_d{hh}"], writes=[f"rb{hh}"])

                def emit_evac_mults(hp, qb, j):
                    for hh in range(2):
                        Pp, kP = PSp[j % 2][hh], f"PSp{'ab'[hh]}{j % 2}"
                        ro = 64 * hh
                        P.op("v", lambda v, Pp=Pp, hp=hp, qb=qb, ro=ro, hh=hh:
                             v.tensor_tensor(
                                 q_fm[hp][ro:ro + 64, 512 * qb:512 * qb + 512],
                                 Pp[0:64, :], rb[hh][:, :], ALU.mult),
                             reads=[kP, f"rb{hh}"],
                             writes=[f"qfm{hp}_{'ab'[hh]}_{qb}"])

                blocks = [(hp, qb) for hp in range(CCH) for qb in range(4)]
                nbk = len(blocks)
                nwj = len(wjobs)
                for j, (hp, qb) in enumerate(blocks):
                    quota = (nwj * (j + 1)) // nbk - (nwj * j) // nbk
                    for kt in range(NT):
                        emit_S(hp, qb, kt)
                        if kt == 0 and j > 0:
                            emit_evac_recips(*blocks[j - 1], j - 1)
                        if kt > 0:
                            emit_PV(hp, kt - 1, j)
                        if kt == 1 and quota >= 1:
                            emit_wjobs(1)
                        if kt == 4 and j > 0:
                            emit_evac_mults(*blocks[j - 1], j - 1)
                        if kt == 7 and quota >= 2:
                            emit_wjobs(1)
                        if kt == 12 and quota >= 3:
                            emit_wjobs(quota - 2)
                    emit_PV(hp, NT - 1, j)
                emit_evac_recips(*blocks[-1], len(blocks) - 1)
                emit_evac_mults(*blocks[-1], len(blocks) - 1)
                emit_wjobs(len(wjobs))
                P.barrier()
            if "ao_d" in dbg:
                ao_d = nc.dram_tensor("dbg_ao_d", [C, T], BF16,
                                      kind="ExternalOutput").ap()
                for c in range(CCH):
                    P.dma("Y", ao_d[128 * c:128 * c + 128, :], q_fm[c][:, :],
                          reads=[f"qfm{c}_{ab}_{qb}" for ab in "ab"
                                 for qb in range(4)],
                          writes=["dbg_ao_d"])
                dbg_outs["ao_d"] = "dbg_ao_d"
            if "negm" in dbg:
                nm_d = nc.dram_tensor("dbg_negm", [80, T], BF16,
                                      kind="ExternalOutput").ap()
                P.dma("Y", nm_d, NM[:, :], reads=["NM"], writes=["dbg_negm"])
                dbg_outs["negm"] = "dbg_negm"
        stKV.close()

        acc = [SB(ctx, f"acc{n}", [128, C], F32, side="right") for n in range(NT)]
        if "D" in phases:
            with ExitStack() as st:
                pwT = [SB(st, f"pwT{c}", [128, C], BF16) for c in range(CCH)]
                PSpj = [PS(st, f"PSpj{i}", [128, 512]) for i in range(2)]
                for c in range(CCH):
                    P.dma("S", pwT[c][:, :], projw_bf[:, 128 * c:128 * c + 128],
                          reads=["projw_bf"], writes=[f"pwT{c}"], transpose=True)
                aokeys = [f"qfm{c}_{ab}_{qb}" for c in range(CCH)
                          for ab in "ab" for qb in range(4)]
                for n in range(NT):
                    for jh in range(2):
                        b = jh
                        kps = f"PSpj{b}"
                        P.group("t", [
                            (lambda t, c=c, n=n, jh=jh, b=b,
                             first=(c == 0), last=(c == CCH - 1): t.matmul(
                                PSpj[b][:, :], q_fm[c][:, 128 * n:128 * n + 128],
                                pwT[c][:, 512 * jh:512 * jh + 512],
                                start=first, stop=last))
                            for c in range(CCH)],
                            reads=aokeys + [f"pwT{c}" for c in range(CCH)],
                            writes=[kps])
                        P.op("v", lambda v, n=n, jh=jh, b=b: v.tensor_tensor(
                            acc[n][:, 512 * jh:512 * jh + 512], PSpj[b][:, :],
                            projb_b[:, 512 * jh:512 * jh + 512], ALU.add),
                            reads=[kps, "projb_b"], writes=[f"acc{n}"])
                P.barrier()
            if "acc_d" in dbg:
                acc_d = nc.dram_tensor("dbg_acc_d", [T, C], F32,
                                       kind="ExternalOutput").ap()
                for n in range(NT):
                    P.dma("Y", acc_d[128 * n:128 * n + 128, :], acc[n][:, :],
                          reads=[f"acc{n}"], writes=["dbg_acc_d"])
                dbg_outs["acc_d"] = "dbg_acc_d"
        stQ.close()

        if "E" in phases:
            with ExitStack() as st:
                al = lambda n, sh, dt: SB(st, n, sh, dt)
                xq_fm = [al(f"xqfm{c}", [128, T], BF16) for c in range(CCH)]
                wgT = [[al(f"wgT{c}_{i}", [128, 512], BF16)
                        for c in range(CCH)] for i in range(2)]
                wuT = [[al(f"wuT{c}_{i}", [128, 512], BF16)
                        for c in range(CCH)] for i in range(2)]
                wdT = [al(f"wdT{m}", [128, C], BF16) for m in range(4)]
                hq_fm = [al(f"hqfm{m}", [128, T], BF16) for m in range(4)]
                sil = [al(f"sil{i}", [128, 512], F32) for i in range(2)]
                h32 = [al(f"h32{i}", [128, 1024], F32) for i in range(2)]
                scrs = quant_scratch2(al, "qE", 1024)
                hqb = [scrs[i]["sg"] for i in range(2)]
                xf = [scrs[0]["s"], scrs[1]["s"]]
                ot = [scrs[0]["aa"], scrs[1]["aa"]]
                PSg = [PS(st, f"PSg{i}", [128, 512]) for i in range(2)]
                PSu = [PS(st, f"PSu{i}", [128, 512]) for i in range(2)]
                PSd = [PS(st, f"PSd{i}", [128, C]) for i in range(2)]
                for c in range(CCH):
                    P.dma("Y", xq_fm[c][:, :], xq_bf[:, 128 * c:128 * c + 128],
                          reads=[f"xq_bf{n}" for n in range(NT)],
                          writes=[f"xqfm{c}"], transpose=True)

                def emit_gup(ib, n):
                    i0, iw = IBLKS[ib]
                    b = n % 2
                    wp = ib % 2
                    P.group("t", [
                        (lambda t, c=c, n=n, iw=iw, i0=i0, b=b, wp=wp,
                         first=(c == 0), last=(c == CCH - 1): t.matmul(
                            PSg[b][:, 0:iw], xq_fm[c][:, 128 * n:128 * n + 128],
                            wgT[wp][c][:, 0:iw], start=first, stop=last))
                        for c in range(CCH)],
                        reads=[f"xqfm{c}" for c in range(CCH)] +
                              [f"wgT{c}_{wp}" for c in range(CCH)],
                        writes=[f"PSg{b}"])
                    P.group("t", [
                        (lambda t, c=c, n=n, iw=iw, i0=i0, b=b, wp=wp,
                         first=(c == 0), last=(c == CCH - 1): t.matmul(
                            PSu[b][:, 0:iw], xq_fm[c][:, 128 * n:128 * n + 128],
                            wuT[wp][c][:, 0:iw], start=first, stop=last))
                        for c in range(CCH)],
                        reads=[f"xqfm{c}" for c in range(CCH)] +
                              [f"wuT{c}_{wp}" for c in range(CCH)],
                        writes=[f"PSu{b}"])
                    pp = (n // 2) % 2
                    ho = iw * (n % 2)
                    P.op("s", lambda sc, iw=iw, b=b: sc.activation(
                        sil[b][:, 0:iw], PSg[b][:, 0:iw], AF.Silu),
                        reads=[f"PSg{b}"], writes=[f"sil{b}"])
                    P.op("v", lambda v, iw=iw, b=b, pp=pp, ho=ho: v.tensor_tensor(
                        h32[pp][:, ho:ho + iw], sil[b][:, 0:iw],
                        PSu[b][:, 0:iw], ALU.mult),
                        reads=[f"sil{b}", f"PSu{b}"], writes=[f"h32{pp}"])

                def emit_down(ib, n):
                    i0, iw = IBLKS[ib]
                    nm = iw // 128
                    b = n % 2
                    kps = f"PSd{b}"
                    P.group("t", [
                        (lambda t, m=m, jh=jh, n=n, b=b,
                         stt=(m == 0), sp=(m == nm - 1): t.matmul(
                            PSd[b][:, 512 * jh:512 * jh + 512],
                            hq_fm[m][:, 128 * n:128 * n + 128],
                            wdT[m][:, 512 * jh:512 * jh + 512],
                            start=stt, stop=sp))
                        for jh in range(2) for m in range(nm)],
                        reads=[f"hqfm{m}" for m in range(nm)] +
                              [f"wdT{m}" for m in range(nm)],
                        writes=[kps])
                    P.op("v", lambda v, n=n, b=b: v.tensor_tensor(
                        acc[n][:, :], acc[n][:, :], PSd[b][:, :], ALU.add),
                        reads=[kps, f"acc{n}"], writes=[f"acc{n}"])

                def emit_hq_tail(ib, p):
                    i0, iw = IBLKS[ib]
                    pp = p % 2
                    emit_quant_tail(P, 2 * iw, f"hqb{pp}", hqb[pp][:, 0:2 * iw],
                                    scrs[pp])
                    for half in range(2):
                        n = 2 * p + half
                        P.dma("G", hq_bf[128 * n:128 * n + 128, i0:i0 + iw],
                              hqb[pp][:, iw * half:iw * half + iw],
                              reads=[f"hqb{pp}"], writes=[f"hq{ib}_{n}"])

                NB_ = len(IBLKS)

                def load_gup_w(ib):
                    i0, iw = IBLKS[ib]
                    wp = ib % 2
                    for c in range(CCH):
                        P.dma("Y", wgT[wp][c][:, 0:iw],
                              wgq_bf[i0:i0 + iw, 128 * c:128 * c + 128],
                              reads=WGK, writes=[f"wgT{c}_{wp}"], transpose=True)
                        P.dma("Y", wuT[wp][c][:, 0:iw],
                              wuq_bf[i0:i0 + iw, 128 * c:128 * c + 128],
                              reads=WUK, writes=[f"wuT{c}_{wp}"], transpose=True)

                load_gup_w(0)
                for ib in range(NB_ + 1):
                    if ib + 1 < NB_:
                        load_gup_w(ib + 1)
                    pend = None
                    for n in range(NT):
                        if ib >= 1 and ib == NB_:
                            emit_down(ib - 1, n)
                        if ib < NB_:
                            emit_gup(ib, n)
                            if ib >= 1 and n >= 2:
                                emit_down(ib - 1, n - 2)
                            if n % 2 == 1:
                                p = n // 2
                                i0, iw = IBLKS[ib]
                                emit_quant_head(P, 2 * iw, f"h32{p % 2}",
                                                h32[p % 2][:, 0:2 * iw],
                                                scrs[p % 2])
                                if pend is not None:
                                    emit_hq_tail(ib, pend)
                                pend = p
                        if ib == NB_:
                            b = n % 2
                            P.dma("S", xf[b][:, :], x[128 * n:128 * n + 128, :],
                                  writes=[f"xf{b}", f"qE{b}s"])
                            P.op("v", lambda v, n=n, b=b: v.tensor_tensor(
                                ot[b][:, :], acc[n][:, :], xf[b][:, :], ALU.add),
                                reads=[f"acc{n}", f"xf{b}"],
                                writes=[f"ot{b}", f"qE{b}aa"])
                            P.dma("Y", out[128 * n:128 * n + 128, :],
                                  ot[b][:, :], reads=[f"ot{b}"],
                                  writes=[f"out{n}"])
                    if 1 <= ib < NB_:
                        emit_down(ib - 1, NT - 2)
                        emit_down(ib - 1, NT - 1)
                    if pend is not None:
                        emit_hq_tail(ib, pend)
                    if ib < NB_:
                        i0, iw = IBLKS[ib]
                        nm = iw // 128
                        for m in range(nm):
                            P.dma("S", wdT[m][:, :],
                                  wdq_bf[:, i0 + 128 * m:i0 + 128 * m + 128],
                                  reads=WDK, writes=[f"wdT{m}"], transpose=True)
                        for m in range(nm):
                            P.dma("Y", hq_fm[m][:, :],
                                  hq_bf[:, i0 + 128 * m:i0 + 128 * m + 128],
                                  reads=[f"hq{ib}_{n}" for n in range(NT)],
                                  writes=[f"hqfm{m}"], transpose=True)
                P.barrier()

    ctx.close()
    return nc, dbg_outs


def _rope_tables():
    n = D // 2
    j = np.arange(n, dtype=np.float32)
    freqs = np.float32(1.0) * (np.float32(40.0) ** (j / np.float32(n - 1)))
    pos = np.arange(T, dtype=np.float32) / np.float32(T)
    theta = np.float32(2.0 * np.pi) * pos[:, None] * freqs[None, :]
    return np.cos(theta).astype(np.float32), np.sin(theta).astype(np.float32)


def _rowind():
    r = np.zeros((80, H * 128), dtype=np.float32)
    for h in range(H):
        r[h, 128 * h:128 * h + 128] = 1.0
        r[64 + h, 128 * h:128 * h + 128] = 1.0
    return r


def make_in_maps(L0_attn_x, L0_attn_qkv_w, L0_attn_qkv_b, L0_attn_proj_w,
                 L0_attn_proj_b, L0_ffn_w_gate, L0_ffn_w_up, L0_ffn_w_down):
    xf = np.ascontiguousarray(np.asarray(L0_attn_x, dtype=np.float32))
    cost, sint = _rope_tables()
    common = {
        "qkv_w": np.ascontiguousarray(np.asarray(L0_attn_qkv_w, np.float32)),
        "proj_w": np.ascontiguousarray(np.asarray(L0_attn_proj_w, np.float32)),
        "wg": np.ascontiguousarray(np.asarray(L0_ffn_w_gate, np.float32)),
        "wu": np.ascontiguousarray(np.asarray(L0_ffn_w_up, np.float32)),
        "wd": np.ascontiguousarray(np.asarray(L0_ffn_w_down, np.float32)),
        "qkvb": np.ascontiguousarray(np.asarray(L0_attn_qkv_b, np.float32)
                                     .reshape(1, 3 * C)),
        "projb": np.ascontiguousarray(np.asarray(L0_attn_proj_b, np.float32)
                                      .reshape(1, C)),
        "costm": cost, "sintm": sint,
        "iden": np.eye(128, dtype=np.float32),
        "rowind": _rowind(),
    }
    return [dict(common, x=np.ascontiguousarray(xf[i])) for i in range(8)]


def kernel(L0_attn_x, L0_attn_qkv_w, L0_attn_qkv_b, L0_attn_proj_w,
           L0_attn_proj_b, L0_ffn_w_gate, L0_ffn_w_up, L0_ffn_w_down):
    in_maps = make_in_maps(L0_attn_x, L0_attn_qkv_w, L0_attn_qkv_b,
                           L0_attn_proj_w, L0_attn_proj_b, L0_ffn_w_gate,
                           L0_ffn_w_up, L0_ffn_w_down)
    nc, _ = build_program()
    res = run_bass_kernel_spmd(nc, in_maps, core_ids=list(range(8)))
    return np.stack([r["out"] for r in res.results], axis=0)
